# version 01 (final)
# baseline (speedup 1.0000x reference)
import contextlib
import numpy as np
import ml_dtypes
import concourse.bass as bass
import concourse.mybir as mybir
from concourse.bass_utils import run_bass_kernel_spmd

F32 = mybir.dt.float32
BF16 = mybir.dt.bfloat16
I32 = mybir.dt.int32
AF = mybir.ActivationFunctionType
ALU = mybir.AluOpType
AX = mybir.AxisListType

NCORE = 8
D = 1024
T = 4096
BATCH = 4
TC = 2048
NCH = D // 128
MAGIC = 12582912.0
EPS = 1e-5
BF = ml_dtypes.bfloat16


def cdiv(a, b):
    return (a + b - 1) // b


class Prog:
    ENGS = ("sync", "gpsimd", "vector", "scalar", "tensor")

    def __init__(self, nc, es):
        self.nc = nc
        self.es = es
        self.streams = {e: [] for e in self.ENGS}
        self.sems = {}
        self.counts = {}

    def sem(self, name):
        if name not in self.sems:
            self.sems[name] = self.es.enter_context(self.nc.semaphore(name))
            self.counts[name] = 0
        return self.sems[name]

    def count(self, name):
        self.sem(name)
        return self.counts[name]

    def op(self, eng, fn, wait=(), inc=None, inc_by=None):
        if inc is not None:
            assert inc_by is not None
            self.sem(inc)
            self.counts[inc] += inc_by
            target = self.counts[inc]
        else:
            target = None
        sems = self.sems
        for s, _ in wait:
            self.sem(s)
        wait = tuple(wait)
        ib = inc_by
        sname = inc

        def g(e):
            for (s, v) in wait:
                e.wait_ge(sems[s], v)
            ins = fn(e)
            if sname is not None:
                ins.then_inc(sems[sname], ib)
        self.streams[eng].append(g)
        return target

    def chain(self, eng, fns, wait=(), inc=None, inc_by=None):
        cs = f"__chain_{eng}"
        for i, fn in enumerate(fns):
            w = list(wait) if i == 0 else [(cs, self.count(cs))]
            last = i == len(fns) - 1
            self.op(eng, fn, wait=w,
                    inc=(inc if last and inc else cs),
                    inc_by=(inc_by if last and inc else 1))

    def wait(self, eng, waits):
        sems = self.sems
        for s, _ in waits:
            self.sem(s)
        waits = tuple(waits)

        def g(e):
            for (s, v) in waits:
                e.wait_ge(sems[s], v)
        self.streams[eng].append(g)

    def finalize(self, blk):
        for name in self.ENGS:
            stream = self.streams[name]

            def make(stream):
                def body(e):
                    for f in stream:
                        f(e)
                return body
            getattr(blk, name)(make(stream))


class LaunchBuilder:
    def __init__(self):
        self.nc = bass.Bass("TRN2", target_bir_lowering=False)
        self.es = contextlib.ExitStack()
        self.P = Prog(self.nc, self.es)
        self.inputs = {}
        self.outputs = []

    def inp(self, name, shape, dtype=F32):
        ap = self.nc.dram_tensor(name, list(shape), dtype, kind="ExternalInput").ap()
        self.inputs[name] = ap
        return ap

    def out(self, name, shape, dtype=F32):
        ap = self.nc.dram_tensor(name, list(shape), dtype, kind="ExternalOutput").ap()
        self.outputs.append(name)
        return ap

    def dram(self, name, shape, dtype=F32, shared=False):
        return self.nc.dram_tensor(
            name, list(shape), dtype, kind="Internal",
            addr_space="Shared" if shared else "Local").ap()

    def sb(self, name, shape, dtype):
        return self.es.enter_context(self.nc.sbuf_tensor(name, list(shape), dtype))

    def ps(self, name, shape, dtype=F32):
        return self.es.enter_context(self.nc.psum_tensor(name, list(shape), dtype))

    def finish(self):
        blk = self.es.enter_context(self.nc.Block())
        self.P.finalize(blk)
        self.es.close()
        return self.nc



def mm_group(P, psum, items, wait_first=(), inc=None, inc_by=1):
    n = len(items)
    for i, (lh, rh, w) in enumerate(items):
        waits = list(wait_first) if i == 0 else []
        waits += list(w)
        P.op("tensor",
             (lambda lh, rh, st, sp: lambda e: e.matmul(
                 psum, lh, rh, start=st, stop=sp))(lh, rh, i == 0, i == n - 1),
             wait=waits,
             inc=inc if i == n - 1 else None,
             inc_by=inc_by if i == n - 1 else None)


class LNHelper:

    def __init__(self, L, tag, nt, inv_n=1.0 / D, share=None, rows_cap=None,
                 bc_dtype=F32, ps_override=None, bc_override=None, tmp_override=None):
        self.L, self.tag, self.nt, self.inv_n = L, tag, nt, inv_n
        self.cap = rows_cap or nt
        if share is None:
            self.rows = L.sb(f"{tag}_rows", [1, 4 * self.cap], F32)
            self.scr = L.dram(f"{tag}_scr", [1, 2 * self.cap], F32)
            if ps_override is not None:
                self.ps_s, self.ps_q = ps_override
            else:
                self.ps_s = L.ps(f"{tag}_ps_s", [1, 512])
                self.ps_q = L.ps(f"{tag}_ps_q", [1, 512])
        else:
            self.rows, self.scr, self.ps_s, self.ps_q = (
                share.rows, share.scr, share.ps_s, share.ps_q)
            self.cap = min(self.cap, share.cap)
        if bc_override is not None:
            self.Rbc, self.Bbc = bc_override
        else:
            self.Rbc = L.sb(f"{tag}_Rbc", [128, nt], bc_dtype)
            self.Bbc = L.sb(f"{tag}_Bbc", [128, nt], bc_dtype)
        if tmp_override is not None:
            self.tmp = tmp_override
        elif share is None:
            self.tmp = L.sb(f"{tag}_tmp", [1, self.cap], F32)
        else:
            self.tmp = share.tmp
        self.done = None

    def _tmp_row(self):
        return self.tmp

    def full(self, P, ones_col, chunk_ap_fn, nchunks, sqbuf, wait_x):
        for w0 in range(0, self.nt, self.cap):
            wnt = min(self.cap, self.nt - w0)
            self.stats_from(P, ones_col, chunk_ap_fn, nchunks, sqbuf, wait_x,
                            t_off=w0, wnt=wnt)
            if w0 > 0:
                P.wait("vector", [self.store_done])
            self.rowmath(P, wnt=wnt)
            self.bcast(P, t_off=w0, wnt=wnt)

    def stats_from(self, P, ones_col, chunk_ap_fn, nchunks, sqbuf, wait_x,
                   t_off=0, wnt=None):
        tag = self.tag
        nt = wnt if wnt is not None else self.nt
        ntb = cdiv(nt, 512)
        sqv, _ = sqbuf
        for tb in range(ntb):
            t0, t1 = t_off + tb * 512, t_off + min((tb + 1) * 512, nt)
            w = t1 - t0
            rt0 = t0 - t_off
            for c in range(nchunks):
                xs = chunk_ap_fn(c, t0, t1)
                i = (t_off // 512 + tb) * nchunks + c
                pp = i % 2
                sq_ap = sqv[:, pp * 512: pp * 512 + w]
                swaits = []
                tcount = P.count(f"{tag}qmm")
                if tcount >= 1 and i >= 2:
                    swaits.append((f"{tag}qmm", tcount - 1))
                if i == 0:
                    swaits += list(wait_x)
                P.op("scalar",
                     (lambda sq_ap, xs: lambda e: e.activation(sq_ap, xs, AF.Square))(sq_ap, xs),
                     wait=swaits, inc=f"{tag}sq", inc_by=1)
                P.op("tensor",
                     (lambda xs, w, st, sp: lambda e: e.matmul(
                         self.ps_s[:, :w], ones_col[:], xs, start=st, stop=sp,
                         skip_group_check=True))(xs, w, c == 0, c == nchunks - 1),
                     wait=list(wait_x) if i == 0 else [])
                P.op("tensor",
                     (lambda sq_ap, w, st, sp: lambda e: e.matmul(
                         self.ps_q[:, :w], ones_col[:], sq_ap, start=st, stop=sp,
                         skip_group_check=True))(sq_ap, w, c == 0, c == nchunks - 1),
                     wait=[(f"{tag}sq", P.count(f"{tag}sq"))],
                     inc=f"{tag}qmm", inc_by=1)
            cap = self.cap
            P.op("vector",
                 (lambda rt0, w: lambda e: e.tensor_copy(
                     self.rows[0:1, rt0:rt0 + w], self.ps_s[:, :w]))(rt0, w),
                 wait=[(f"{tag}qmm", P.count(f"{tag}qmm"))], inc=f"{tag}ev", inc_by=1)
            P.op("vector",
                 (lambda rt0, w: lambda e: e.tensor_copy(
                     self.rows[0:1, cap + rt0:cap + rt0 + w], self.ps_q[:, :w]))(rt0, w),
                 inc=f"{tag}ev", inc_by=1)
            P.wait("tensor", [(f"{tag}ev", P.count(f"{tag}ev"))])

    def rowmath(self, P, wnt=None):
        tag, inv_n = self.tag, self.inv_n
        nt = wnt if wnt is not None else self.nt
        cap = self.cap
        r = self.rows
        SUM = lambda: r[0:1, 0:nt]
        SQ = lambda: r[0:1, cap:cap + nt]
        RR = lambda: r[0:1, 2 * cap:2 * cap + nt]
        BB = lambda: r[0:1, 3 * cap:3 * cap + nt]
        TMP = lambda: self.tmp[0:1, 0:nt]
        P.chain("vector", [
            lambda e: e.tensor_scalar_mul(SUM(), SUM(), inv_n),
            lambda e: e.tensor_mul(TMP(), SUM(), SUM()),
            lambda e: e.scalar_tensor_tensor(
                TMP(), SQ(), inv_n, TMP(), ALU.mult, ALU.subtract),
            lambda e: e.tensor_scalar_add(TMP(), TMP(), EPS),
        ], wait=[(f"{tag}ev", P.count(f"{tag}ev"))], inc=f"{tag}rm", inc_by=1)
        P.op("scalar", lambda e: e.activation(RR(), TMP(), AF.Sqrt),
            wait=[(f"{tag}rm", P.count(f"{tag}rm"))], inc=f"{tag}rs", inc_by=1)
        P.chain("vector", [
            lambda e: e.reciprocal(RR(), RR()),
            lambda e: e.scalar_tensor_tensor(
                BB(), SUM(), -1.0, RR(), ALU.mult, ALU.mult),
        ], wait=[(f"{tag}rs", P.count(f"{tag}rs"))], inc=f"{tag}rm", inc_by=1)

    def bcast(self, P, t_off=0, wnt=None):
        tag = self.tag
        nt = wnt if wnt is not None else self.nt
        cap = self.cap
        r = self.rows
        P.op("sync", (lambda nt: lambda e: e.dma_start(
            out=self.scr[0:1, :nt], in_=r[0:1, 2 * cap:2 * cap + nt]))(nt),
             wait=[(f"{tag}rm", P.count(f"{tag}rm"))], inc=f"{tag}bd", inc_by=16)
        P.op("sync", (lambda nt: lambda e: e.dma_start(
            out=self.scr[0:1, cap:cap + nt], in_=r[0:1, 3 * cap:3 * cap + nt]))(nt),
             inc=f"{tag}bd", inc_by=16)
        self.store_done = (f"{tag}bd", P.count(f"{tag}bd"))
        _rbc_ap = self.Rbc[:] if hasattr(self.Rbc, "ap") else self.Rbc
        cast = _rbc_ap.dtype != F32
        eng = "gpsimd" if cast else "sync"
        sem = f"{tag}bg" if cast else f"{tag}bd"
        P.op(eng, (lambda t_off, nt: lambda e: e.dma_start(
            out=self.Rbc[:, t_off:t_off + nt],
            in_=self.scr[0:1, :nt].broadcast_to((128, nt))))(t_off, nt),
            wait=[(f"{tag}bd", P.count(f"{tag}bd"))], inc=sem, inc_by=16)
        P.op(eng, (lambda t_off, nt: lambda e: e.dma_start(
            out=self.Bbc[:, t_off:t_off + nt],
            in_=self.scr[0:1, cap:cap + nt].broadcast_to((128, nt))))(t_off, nt),
            inc=sem, inc_by=16)
        self.done = (sem, P.count(sem))

    def apply(self, P, src_ap, dst_ap, t0, t1, extra_wait=(), inc=None, inc_by=1):
        w = [self.done] + list(extra_wait)
        P.chain("vector", [
            (lambda: lambda e: e.tensor_mul(dst_ap, src_ap, self.Rbc[:, t0:t1]))(),
            (lambda: lambda e: e.tensor_add(dst_ap, dst_ap, self.Bbc[:, t0:t1]))(),
        ], wait=w, inc=inc, inc_by=inc_by)



def build_ffn0(upto=None, mmflags=("sq", "stats")):
    L = LaunchBuilder()
    P = L.P
    x1 = L.inp("x1", [D, TC], F32)
    kern = L.inp("kern", [D, D], F32)
    biasv = L.inp("biasv", [1, D], F32)
    amat = L.inp("amat", [D, 16], F32)
    bmat = L.inp("bmat", [16, D], F32)
    x2 = L.out("x2", [D, TC], F32)

    xb = L.sb("xb", [128, NCH * TC], BF16)
    kf_t = L.sb("kf_t", [128, NCH * D], F32)
    ysb_t = L.sb("ysb_t", [128, NCH * TC], BF16)
    kf = kf_t.ap()
    ysb = ysb_t.ap()
    wqb = L.sb("wqb", [128, NCH * D], BF16)
    ab = L.sb("ab", [128, NCH * 16], BF16)
    bbm = L.sb("bbm", [16, D], BF16)
    bias_b = L.sb("bias_b", [1, D], BF16)
    t1b = L.sb("t1b", [16, 512], BF16)
    sqb = L.sb("sqb", [128, 1024], BF16)
    mx = L.sb("mx", [128, 16], F32)
    sc = L.sb("sc", [1, 8], F32)
    scb2 = L.sb("scb2", [128, 2], F32)
    oc = L.sb("oc", [128, 1], BF16)
    ot = L.sb("ot", [1, 512], BF16)
    xadd = L.sb("xadd", [128, TC], F32)
    x2st = L.sb("x2st", [128, TC], F32)

    scr1 = L.dram("scr1", [1, 2], F32)
    scr128 = L.dram("scr128", [128, 1], F32)
    mxrow = L.sb("mxrow", [1, 128], F32)

    ps_t1 = L.ps("ps_t1", [16, 512])
    ps_y = L.ps("ps_y", [128, 1024])
    ln_x = LNHelper(L, "lnx", TC, bc_dtype=BF16)
    ln_y = LNHelper(L, "lny", TC, share=ln_x, bc_dtype=BF16)

    for c in range(NCH):
        P.op("gpsimd", (lambda c: lambda e: e.dma_start(
            out=xb[:, c * TC:(c + 1) * TC], in_=x1[c * 128:(c + 1) * 128, :]))(c),
            inc="ldx", inc_by=16)
        P.op("sync", (lambda c: lambda e: e.dma_start(
            out=kf[:, c * D:(c + 1) * D], in_=kern[c * 128:(c + 1) * 128, :]))(c),
            inc="ldk", inc_by=16)
        P.op("gpsimd", (lambda c: lambda e: e.dma_start(
            out=ab[:, c * 16:(c + 1) * 16], in_=amat[c * 128:(c + 1) * 128, :]))(c),
            inc="ldw", inc_by=16)
    P.op("gpsimd", lambda e: e.dma_start(out=bbm[:], in_=bmat), inc="ldw", inc_by=16)
    P.op("gpsimd", lambda e: e.dma_start(out=bias_b[:], in_=biasv), inc="ldw", inc_by=16)
    LDX, LDK, LDW = P.count("ldx"), P.count("ldk"), P.count("ldw")

    P.op("vector", lambda e: e.memset(oc[:], 1.0), inc="vi", inc_by=1)
    P.op("vector", lambda e: e.memset(ot[:], 1.0), inc="vi", inc_by=1)
    VI = P.count("vi")

    for c in range(NCH):
        P.op("vector", (lambda c: lambda e: e.reduce_max(
            mx[:, c:c + 1], kf[:, c * D:(c + 1) * D], axis=AX.X,
            apply_absolute_value=True))(c),
            wait=[("ldk", LDK)] if c == 0 else [], inc="q1", inc_by=1)
    P.op("vector", lambda e: e.reduce_max(mx[:, 8:9], mx[:, 0:NCH], axis=AX.X),
         wait=[("q1", P.count("q1"))], inc="q1", inc_by=1)
    P.op("sync", lambda e: e.dma_start(out=scr128, in_=mx[:, 8:9]),
         wait=[("q1", P.count("q1"))], inc="qb", inc_by=16)
    P.op("sync", lambda e: e.dma_start(out=mxrow[:], in_=scr128.rearrange("a b -> b a")),
         wait=[("qb", P.count("qb"))], inc="qb", inc_by=16)
    P.chain("vector", [
        lambda e: e.reduce_max(sc[:, 0:1], mxrow[:], axis=AX.X),
        lambda e: e.tensor_scalar_mul(sc[:, 1:2], sc[:, 0:1], 1.0 / 7.0),
        lambda e: e.reciprocal(sc[:, 2:3], sc[:, 1:2]),
    ], wait=[("qb", P.count("qb"))], inc="q3", inc_by=1)
    P.op("sync", lambda e: e.dma_start(out=scr1, in_=sc[:, 1:3]),
         wait=[("q3", P.count("q3"))], inc="qd", inc_by=16)
    P.op("sync", lambda e: e.dma_start(out=scb2[:], in_=scr1.broadcast_to((128, 2))),
         wait=[("qd", P.count("qd"))], inc="qd", inc_by=16)
    QD = P.count("qd")
    for c in range(NCH):
        sl = (lambda c: kf[:, c * D:(c + 1) * D])(c)
        P.chain("vector", [
            (lambda sl: lambda e: e.tensor_scalar(
                sl, sl, scb2[:, 1:2], MAGIC, ALU.mult, ALU.add))(sl),
            (lambda sl: lambda e: e.tensor_scalar(
                sl, sl, MAGIC, -8.0, ALU.subtract, ALU.max))(sl),
            (lambda sl: lambda e: e.tensor_scalar(
                sl, sl, 7.0, scb2[:, 0:1], ALU.min, ALU.mult))(sl),
            (lambda c, sl: lambda e: e.tensor_copy(
                wqb[:, c * D:(c + 1) * D], sl))(c, sl),
        ], wait=[("qd", QD)] if c == 0 else [], inc="q4", inc_by=1)
    WQ = P.count("q4")
    if upto == "quant":
        tap = L.out("tap", [128, NCH * D], BF16)
        P.op("sync", lambda e: e.dma_start(out=tap, in_=wqb[:]),
             wait=[("q4", WQ)], inc="tp", inc_by=16)
        P.wait("sync", [("tp", 16)])
        return L

    def xchunk(c, t0, t1):
        return xb[:, c * TC + t0: c * TC + t1]
    ln_x.stats_from(P, oc, xchunk, NCH, (sqb, "free"), [("ldx", LDX), ("vi", VI)])
    ln_x.rowmath(P)
    ln_x.bcast(P)
    if upto == "stats":
        tap = L.out("tap", [128, TC], F32)
        P.op("sync", lambda e: e.dma_start(out=tap, in_=ln_x.Rbc[:]),
             wait=[ln_x.done], inc="tp", inc_by=16)
        P.wait("sync", [("tp", 16)])
        return L
    for c in range(NCH):
        ln_x.apply(P, xb[:, c * TC:(c + 1) * TC], xb[:, c * TC:(c + 1) * TC],
                   0, TC, inc="xn", inc_by=1)
    XN = P.count("xn")
    if upto == "xn":
        tap = L.out("tap", [128, NCH * TC], BF16)
        P.op("sync", lambda e: e.dma_start(out=tap, in_=xb[:]),
             wait=[("xn", XN)], inc="tp", inc_by=16)
        P.wait("sync", [("tp", 16)])
        return L

    NTB = TC // 512
    for tb in range(NTB):
        t0 = tb * 512
        items = [(ab[:, c * 16:(c + 1) * 16], xb[:, c * TC + t0:c * TC + t0 + 512],
                  [("xn", XN), ("ldw", LDW)] if c == 0 else []) for c in range(NCH)]
        wf_t1 = [("t1ev", P.count("t1ev"))] if tb else []
        mm_group(P, ps_t1[:], items, wait_first=wf_t1, inc="t1mm", inc_by=1)
        P.op("vector", lambda e: e.tensor_copy(t1b[:], ps_t1[:]),
             wait=[("t1mm", P.count("t1mm"))] + ([("ymm", P.count("ymm"))] if tb else []),
             inc="t1ev", inc_by=1)
        T1EV = P.count("t1ev")
        for fo in range(NCH):
            pp = fo % 2
            pslice = ps_y[:, pp * 512:(pp + 1) * 512]
            items = [(wqb[:, c * D + fo * 128:c * D + (fo + 1) * 128],
                      xb[:, c * TC + t0:c * TC + t0 + 512],
                      [("q4", WQ)] if (tb == 0 and fo == 0 and c == 0) else [])
                     for c in range(NCH)]
            items.append((bbm[:, fo * 128:(fo + 1) * 128], t1b[:], [("t1ev", T1EV)]))
            items.append((bias_b[0:1, fo * 128:(fo + 1) * 128], ot[:], []))
            gid = tb * NCH + fo
            wf = []
            if gid >= 2:
                wf = [("yev", gid - 1)]
            mm_group(P, pslice, items, wait_first=wf, inc="ymm", inc_by=1)
            YMM = P.count("ymm")
            P.op("vector", (lambda fo, t0, pslice: lambda e: e.tensor_copy(
                ysb[:, fo * TC + t0: fo * TC + t0 + 512], pslice))(fo, t0, pslice),
                wait=[("ymm", YMM)], inc="yev", inc_by=1)
            if "sq" in mmflags:
                P.op("scalar", (lambda gid, fo, t0: lambda e: e.activation(
                    sqb[:, (gid % 2) * 512:(gid % 2) * 512 + 512],
                    ysb[:, fo * TC + t0: fo * TC + t0 + 512], AF.Square))(gid, fo, t0),
                    wait=[("yev", P.count("yev"))] + ([("ysqmm", max(0, P.count("ysqmm") - 1))] if P.count("ysqmm") >= 1 else []),
                    inc="ysq", inc_by=1)
            if "stats" in mmflags:
                P.op("tensor", (lambda gid, fo: lambda e: e.matmul(
                    ln_y.ps_q[:], oc[:], sqb[:, (gid % 2) * 512:(gid % 2) * 512 + 512],
                    start=(fo == 0), stop=(fo == NCH - 1), skip_group_check=True))(gid, fo),
                    wait=[("ysq", P.count("ysq"))], inc="ysqmm", inc_by=1)
                P.op("tensor", (lambda tb, fo, t0: lambda e: e.matmul(
                    ln_y.ps_s[:], oc[:], ysb[:, fo * TC + t0: fo * TC + t0 + 512],
                    start=(fo == 0), stop=(fo == NCH - 1), skip_group_check=True))(tb, fo, t0),
                    wait=[("yev", P.count("yev"))], inc="ysmm", inc_by=1)
        if "stats" in mmflags:
            P.op("vector", (lambda t0: lambda e: e.tensor_copy(
                ln_y.rows[0:1, t0:t0 + 512], ln_y.ps_s[:]))(t0),
                wait=[("ysmm", P.count("ysmm")), ("ysqmm", P.count("ysqmm"))],
                inc="ystev", inc_by=1)
            P.op("vector", (lambda t0: lambda e: e.tensor_copy(
                ln_y.rows[0:1, TC + t0:TC + t0 + 512], ln_y.ps_q[:]))(t0),
                inc="ystev", inc_by=1)
            if tb + 1 < NTB:
                P.wait("tensor", [("ystev", P.count("ystev"))])

    if upto == "mm":
        tap = L.out("tap", [128, NCH * TC], BF16)
        P.op("sync", lambda e: e.dma_start(out=tap, in_=ysb[:]),
             wait=[("yev", P.count("yev"))], inc="tp", inc_by=16)
        P.wait("sync", [("tp", 16)])
        return L

    P.op("vector", lambda e: e.engine_nop(), wait=[("ystev", P.count("ystev"))],
         inc="lnyev", inc_by=1)
    ln_y.rowmath(P)
    ln_y.bcast(P)


    for c in range(NCH):
        P.op("sync", (lambda c: lambda e: e.dma_start(
            out=xadd[:], in_=x1[c * 128:(c + 1) * 128, :]))(c),
            wait=[("cmb", P.count("cmb"))] if c >= 1 else [],
            inc="ldr", inc_by=16)
        LDR = P.count("ldr")
        P.chain("vector", [
            (lambda c: lambda e: e.tensor_mul(
                x2st[:], ysb[:, c * TC:(c + 1) * TC], ln_y.Rbc[:]))(c),
            lambda e: e.tensor_add(x2st[:], x2st[:], ln_y.Bbc[:]),
            (lambda LDR: lambda e: (e.wait_ge(P.sems["ldr"], LDR),
                                    e.tensor_add(x2st[:], x2st[:], xadd[:]))[1])(LDR),
        ], wait=[ln_y.done] + ([("xst", P.count("xst"))] if c >= 1 else []),
            inc="cmb", inc_by=1)
        P.op("sync", (lambda c: lambda e: e.dma_start(
            out=x2[c * 128:(c + 1) * 128, :], in_=x2st[:]))(c),
            wait=[("cmb", P.count("cmb"))], inc="xst", inc_by=16)
    P.wait("sync", [("xst", P.count("xst"))])

    return L





def build_attn0(upto=None):
    L = LaunchBuilder()
    P = L.P
    NTT = 16
    TCTX = T
    xc = L.inp("xc", [D, TCTX], F32)
    xmy = L.inp("xmy", [D, TC], F32)
    qkvT = L.inp("qkvT", [D, 3 * D], F32)
    outT = L.inp("outT", [D, D], F32)
    hexT = L.inp("hexT", [D, 6], F32)
    q6T = L.inp("q6T", [D, 6], F32)
    trf = L.inp("trf", [7, 36], F32)
    tw = L.inp("tw", [16, 7], F32)
    smalls = L.inp("smalls", [1, 4], F32)
    masksI = L.inp("masksI", [128, 4 * 256], F32)
    x1t = L.out("x1t", [D, TC], F32)

    xn = L.sb("xn", [128, NCH * TCTX], BF16)
    xnf = xn.ap().bitcast(F32)
    y_sb = xn.ap()
    outb = L.sb("outb", [128, 2 * NCH * 128], BF16)
    whb = L.sb("whb", [128, NCH * 192], BF16)
    sqb = L.sb("sqb", [128, 1024], BF16)
    mk = L.sb("mk", [128, 4 * 256], BF16)
    sb_t = L.sb("sb_t", [6, TCTX], BF16)
    hexc = L.sb("hexc", [6, TCTX], BF16)
    sbq = L.sb("sbq", [6, TC], BF16)
    exth = L.sb("exth", [6, TC], BF16)
    q_h = L.sb("q_h", [64, TC], BF16)
    k_h = L.sb("k_h", [64, TCTX], BF16)
    vt = L.sb("vt", [128, 32 * 65], BF16)
    ao = L.sb("ao", [128, NCH * TC], BF16)
    sc_sb = L.sb("sc_sb", [128, 2 * 256], F32)
    t_sb = L.sb("t_sb", [128, 2 * 256], F32)
    ex_sb = L.sb("ex_sb", [128, 2 * 256], F32)
    s2_sb = L.sb("s2_sb", [128, 2 * 256], F32)
    pt_sb = L.sb("pt_sb", [128, 2 * 256], BF16)
    po_sb = L.sb("po_sb", [64, 256], BF16)
    nrow = L.sb("nrow", [1, 256], F32)
    nrec = L.sb("nrec", [1, 256], BF16)
    o64 = L.sb("o64", [1, 64], BF16)
    oc = L.sb("oc", [128, 1], BF16)
    smw = L.sb("smw", [16, 16], F32)
    smT = L.sb("smT", [7, 16], BF16)
    Mh_sb = L.sb("Mh_sb", [6, 96], BF16)
    wsc = L.sb("wsc", [1, 16], F32)
    smallB = L.sb("smallB", [128, 8], F32)
    tmp6 = L.sb("tmp6", [6, 512], BF16)
    trb = L.sb("trb", [7, 36], BF16)
    twb = L.sb("twb", [16, 7], F32)

    scrS = L.dram("scrS", [1, 8], F32)
    scrT = L.dram("scrT", [16, 7], F32)

    ps_sc = L.ps("ps_sc", [128, 512])
    ps_ex = L.ps("ps_ex", [128, 512])
    ps_pv = L.ps("ps_pv", [65, 256])
    ps_v = L.ps("ps_v", [128, 64])
    ps_pj = L.ps("ps_pj", [128, 512])

    ln_x = LNHelper(L, "lnx", TCTX, rows_cap=1024, bc_dtype=BF16,
                    ps_override=(ps_sc[0:1, 0:512], ps_ex[0:1, 0:512]))
    ln_y = LNHelper(L, "lny", TC, share=ln_x, bc_dtype=BF16,
                    bc_override=(ln_x.Rbc[:, 0:TC], ln_x.Bbc[:, 0:TC]))

    p = P.op

    for c in range(NCH):
        p("gpsimd", (lambda c: lambda e: e.dma_start(
            out=xn[:, c * TCTX:(c + 1) * TCTX], in_=xc[c * 128:(c + 1) * 128, :]))(c),
            inc="gx", inc_by=16)
    p("gpsimd", lambda e: e.dma_start(out=mk[:], in_=masksI), inc="gw", inc_by=16)
    p("gpsimd", lambda e: e.dma_start(out=trb[:], in_=trf), inc="gw", inc_by=16)
    p("sync", lambda e: e.dma_start(out=twb[:], in_=tw), inc="sk", inc_by=16)
    p("sync", lambda e: e.dma_start(out=wsc[0:1, 0:4], in_=smalls), inc="sk", inc_by=16)
    hb = L.sb("hb", [128, NCH * 12], BF16)
    for c in range(NCH):
        p("gpsimd", (lambda c: lambda e: e.dma_start(
            out=hb[:, c * 12:c * 12 + 6], in_=hexT[c * 128:(c + 1) * 128, :]))(c),
            inc="gw", inc_by=16)
        p("gpsimd", (lambda c: lambda e: e.dma_start(
            out=hb[:, c * 12 + 6:c * 12 + 12], in_=q6T[c * 128:(c + 1) * 128, :]))(c),
            inc="gw", inc_by=16)
    GX, GW, SK = P.count("gx"), P.count("gw"), P.count("sk")

    p("vector", lambda e: e.memset(oc[:], 1.0), inc="vi", inc_by=1)
    p("vector", lambda e: e.memset(o64[:], 1.0), inc="vi", inc_by=1)
    p("vector", lambda e: e.memset(vt[:], 1.0), inc="vi", inc_by=1)
    VI = P.count("vi")

    def xchunk(c, t0, t1):
        return xn[:, c * TCTX + t0: c * TCTX + t1]
    ln_x.full(P, oc, xchunk, NCH, (sqb, "free"), [("gx", GX), ("vi", VI)])
    for c in range(NCH):
        ln_x.apply(P, xn[:, c * TCTX:(c + 1) * TCTX], xn[:, c * TCTX:(c + 1) * TCTX],
                   0, TCTX, inc="xnp", inc_by=1)
    XN = P.count("xnp")
    if upto == "ln":
        tap = L.out("tap", [128, NCH * TCTX], BF16)
        p("sync", lambda e: e.dma_start(out=tap, in_=xn[:]),
          wait=[("xnp", XN)], inc="tp", inc_by=16)
        P.wait("sync", [("tp", 16)])
        return L

    p("scalar", lambda e: e.activation(wsc[0:1, 4:6], wsc[0:1, 0:2], AF.Sigmoid),
      wait=[("sk", SK)], inc="sm1", inc_by=1)
    p("scalar", lambda e: e.activation(wsc[0:1, 6:7], wsc[0:1, 2:3], AF.Abs),
      inc="sm1", inc_by=1)
    P.chain("vector", [
        lambda e: e.tensor_scalar(wsc[0:1, 7:8], wsc[0:1, 4:5], 3.0, None, ALU.mult),
        lambda e: e.scalar_tensor_tensor(wsc[0:1, 7:8], wsc[0:1, 5:6], 2.0, wsc[0:1, 7:8], ALU.mult, ALU.add),
        lambda e: e.tensor_add(wsc[0:1, 7:8], wsc[0:1, 7:8], wsc[0:1, 6:7]),
        lambda e: e.tensor_scalar(wsc[0:1, 8:9], wsc[0:1, 7:8], -1.0, None, ALU.mult),
        lambda e: e.reciprocal(wsc[0:1, 9:10], wsc[0:1, 3:4]),
        lambda e: e.tensor_scalar(wsc[0:1, 10:11], wsc[0:1, 4:5], 0.5, None, ALU.mult),
        lambda e: e.tensor_scalar(wsc[0:1, 11:12], wsc[0:1, 5:6], 1.0 / 3.0, None, ALU.mult),
        lambda e: e.tensor_copy(wsc[0:1, 12:13], wsc[0:1, 2:3]),
    ], wait=[("sm1", P.count("sm1"))], inc="sm2", inc_by=1)
    p("sync", lambda e: e.dma_start(out=scrS[0:1, 0:1], in_=wsc[0:1, 2:3]),
      wait=[("sm2", P.count("sm2"))], inc="smd", inc_by=16)
    p("sync", lambda e: e.dma_start(out=scrS[0:1, 1:2], in_=wsc[0:1, 8:9]), inc="smd", inc_by=16)
    p("sync", lambda e: e.dma_start(out=scrS[0:1, 2:3], in_=wsc[0:1, 9:10]), inc="smd", inc_by=16)
    p("sync", lambda e: e.dma_start(out=scrS[0:1, 3:4], in_=wsc[0:1, 10:11]), inc="smd", inc_by=16)
    p("sync", lambda e: e.dma_start(out=scrS[0:1, 4:5], in_=wsc[0:1, 11:12]), inc="smd", inc_by=16)
    p("sync", lambda e: e.dma_start(out=smallB[:, 0:5], in_=scrS[0:1, 0:5].broadcast_to((128, 5))),
      wait=[("smd", P.count("smd"))], inc="smd", inc_by=16)
    SMB = P.count("smd")

    for tb in range(TCTX // 512):
        t0 = tb * 512
        for which, dst in ((0, sb_t), (1, hexc)):
            items = [(hb[:, c * 12 + which * 6: c * 12 + which * 6 + 6],
                      xn[:, c * TCTX + t0: c * TCTX + t0 + 512],
                      [("xnp", XN), ("gw", GW)] if (tb == 0 and which == 0 and c == 0) else [])
                     for c in range(NCH)]
            mm_group(P, ps_pj[0:6, 0:512], items,
                     wait_first=[("sbev", P.count("sbev"))] if (tb or which) else [],
                     inc="sbmm", inc_by=1)
            p("vector", (lambda: lambda e: e.tensor_copy(tmp6[:, 0:512], ps_pj[0:6, 0:512]))(),
              wait=[("sbmm", P.count("sbmm"))], inc="sbc", inc_by=1)
            p("scalar", (lambda dst, t0: lambda e: e.activation(
                dst[:, t0:t0 + 512], tmp6[:, 0:512], AF.Tanh))(dst, t0),
              wait=[("sbc", P.count("sbc"))], inc="sbev", inc_by=1)
    SBEV = P.count("sbev")
    for j in range(NTT):
        p("vector", (lambda j: lambda e: e.tensor_scalar(
            sbq[:, j * 128:(j + 1) * 128], sb_t[:, (2 * j + 1) * 128:(2 * j + 2) * 128],
            smallB[0:6, 3:4], None, ALU.mult))(j),
          wait=[("sbev", SBEV), ("smd", SMB)] if j == 0 else [], inc="sbq", inc_by=1)
    SBQ = P.count("sbq")
    if upto == "sb":
        tap = L.out("tap", [6, 2 * TCTX + TC], BF16)
        p("sync", lambda e: e.dma_start(out=tap[:, 0:TCTX], in_=sb_t[:]),
          wait=[("sbq", SBQ)], inc="tp", inc_by=16)
        p("sync", lambda e: e.dma_start(out=tap[:, TCTX:2 * TCTX], in_=hexc[:]), inc="tp", inc_by=16)
        p("sync", lambda e: e.dma_start(out=tap[:, 2 * TCTX:], in_=sbq[:]), inc="tp", inc_by=16)
        P.wait("sync", [("tp", P.count("tp"))])
        return L

    P.chain("vector", [
        lambda e: e.tensor_copy(smw[:, 0:7], twb[:]),
        lambda e: e.reduce_max(smw[:, 7:8], smw[:, 0:7], axis=AX.X),
        lambda e: e.tensor_scalar(smw[:, 0:7], smw[:, 0:7], smw[:, 7:8], None, ALU.subtract),
    ], wait=[("sk", SK)], inc="tw1", inc_by=1)
    p("scalar", lambda e: e.activation(smw[:, 0:7], smw[:, 0:7], AF.Exp, scale=2.0),
      wait=[("tw1", P.count("tw1"))], inc="tw2", inc_by=1)
    P.chain("vector", [
        lambda e: e.reduce_sum(smw[:, 7:8], smw[:, 0:7], axis=AX.X),
        lambda e: e.reciprocal(smw[:, 7:8], smw[:, 7:8]),
        lambda e: e.tensor_scalar(smw[:, 0:7], smw[:, 0:7], smw[:, 7:8], None, ALU.mult),
        lambda e: e.tensor_scalar(smw[:, 0:7], smw[:, 0:7], smallB[0:16, 4:5], None, ALU.mult),
    ], wait=[("tw2", P.count("tw2")), ("smd", SMB)], inc="tw3", inc_by=1)
    p("sync", lambda e: e.dma_start(out=scrT, in_=smw[:, 0:7]),
      wait=[("tw3", P.count("tw3"))], inc="twd", inc_by=16)
    p("gpsimd", lambda e: e.dma_start(out=smT[:], in_=scrT.rearrange("a b -> b a"),
                                      allow_slow_non_contiguous=True),
      wait=[("twd", P.count("twd"))], inc="twg", inc_by=16)
    TWG = P.count("twg")
    for ei in range(6):
        p("tensor", (lambda ei: lambda e: e.matmul(
            ps_v[0:6, 0:16], trb[:, ei:36:6], smT[:], start=True, stop=True))(ei),
          wait=[("twg", TWG), ("gw", GW)] if ei == 0 else [("mhc", P.count("mhc"))],
          inc="mhm", inc_by=1)
        p("vector", (lambda ei: lambda e: e.tensor_copy(
            Mh_sb[:, ei:96:6], ps_v[0:6, 0:16]))(ei),
          wait=[("mhm", P.count("mhm"))], inc="mhc", inc_by=1)
    MHC = P.count("mhc")

    NKCMAX = 32
    for h in range(16):
        hw = []
        for c in range(NCH):
            p("gpsimd", (lambda c, h: lambda e: e.dma_start(
                out=whb[:, c * 192:c * 192 + 64],
                in_=qkvT[c * 128:(c + 1) * 128, h * 64:(h + 1) * 64]))(c, h),
                wait=[("hpj", P.count("hpj"))] if (h > 0 and c == 0) else [],
                inc="ghw", inc_by=16)
            p("gpsimd", (lambda c, h: lambda e: e.dma_start(
                out=whb[:, c * 192 + 64:c * 192 + 128],
                in_=qkvT[c * 128:(c + 1) * 128, D + h * 64:D + (h + 1) * 64]))(c, h),
                inc="ghw", inc_by=16)
            p("gpsimd", (lambda c, h: lambda e: e.dma_start(
                out=whb[:, c * 192 + 128:c * 192 + 192],
                in_=qkvT[c * 128:(c + 1) * 128, 2 * D + h * 64:2 * D + (h + 1) * 64]))(c, h),
                inc="ghw", inc_by=16)
        GHW = P.count("ghw")
        for j in range(NTT):
            p("tensor", (lambda j, h: lambda e: e.matmul(
                ps_ex[0:6, 0:128], Mh_sb[:, h * 6:(h + 1) * 6],
                hexc[:, (2 * j + 1) * 128:(2 * j + 2) * 128], start=True, stop=True))(j, h),
              wait=([("mhc", MHC), ("sbev", SBEV)] if (h == 0 and j == 0) else [])
                   + ([("exc", P.count("exc"))] if (h or j) else []),
              inc="exm", inc_by=1)
            p("vector", (lambda j: lambda e: e.tensor_copy(
                exth[:, j * 128:(j + 1) * 128], ps_ex[0:6, 0:128]))(j),
              wait=[("exm", P.count("exm"))], inc="exc", inc_by=1)
        EXC = P.count("exc")
        for j in range(NTT):
            items = [(whb[:, c * 192:c * 192 + 64],
                      xn[:, c * TCTX + (2 * j + 1) * 128: c * TCTX + (2 * j + 2) * 128],
                      [("ghw", GHW)] if c == 0 and j == 0 else [])
                     for c in range(NCH)]
            mm_group(P, ps_pj[0:64, 0:128], items,
                     wait_first=[("qev", P.count("qev"))] if j else
                                ([("pjfree", P.count("pjfree"))] if h else []),
                     inc="qmm", inc_by=1)
            p("vector", (lambda j: lambda e: e.tensor_copy(
                q_h[:, j * 128:(j + 1) * 128], ps_pj[0:64, 0:128]))(j),
              wait=[("qmm", P.count("qmm"))], inc="qev", inc_by=1)
        for tb in range(TCTX // 512):
            items = [(whb[:, c * 192 + 64:c * 192 + 128],
                      xn[:, c * TCTX + tb * 512: c * TCTX + (tb + 1) * 512], [])
                     for c in range(NCH)]
            mm_group(P, ps_pj[0:64, 0:512], items,
                     wait_first=[("kev" if tb else "qev", P.count("kev" if tb else "qev"))],
                     inc="kmm", inc_by=1)
            p("vector", (lambda tb: lambda e: e.tensor_copy(
                k_h[:, tb * 512:(tb + 1) * 512], ps_pj[0:64, 0:512]))(tb),
              wait=[("kmm", P.count("kmm"))], inc="kev", inc_by=1)
        for tc in range(32):
            items = [(xn[:, c * TCTX + tc * 128: c * TCTX + (tc + 1) * 128],
                      whb[:, c * 192 + 128:c * 192 + 192], [])
                     for c in range(NCH)]
            mm_group(P, ps_v[:, 0:64], items,
                     wait_first=[("vev", P.count("vev"))] if tc else
                                [("nrm", P.count("nrm"))] if h else [],
                     inc="vmm", inc_by=1)
            p("vector", (lambda tc: lambda e: e.tensor_copy(
                vt[:, tc * 65:tc * 65 + 64], ps_v[:, 0:64]))(tc),
              wait=[("vmm", P.count("vmm"))], inc="vev", inc_by=1)
        p("tensor", lambda e: e.engine_nop(), wait=[]) if False else None
        P.wait("tensor", [("vev", P.count("vev")), ("exc", EXC)])
        p("tensor", lambda e: e.matmul(ps_pj[0:1, 0:1], oc[0:1, 0:1], oc[0:1, 0:1],
                                       start=True, stop=True), inc="hpj", inc_by=1)
        for j2 in range(8):
            nkc = 4 * j2 + 4
            tq0 = j2 * 256
            for kc in range(nkc):
                pp = kc % 2
                scs = (lambda pp: ps_sc[:, pp * 256:(pp + 1) * 256])(pp)
                exs = (lambda pp: ps_ex[:, pp * 256:(pp + 1) * 256])(pp)
                wqk = [("scev", max(0, P.count("scev") - 1))] if P.count("scev") >= 1 else []
                p("tensor", (lambda kc, tq0, scs: lambda e: e.matmul(
                    scs, k_h[:, kc * 128:(kc + 1) * 128], q_h[:, tq0:tq0 + 256],
                    start=True, stop=True, skip_group_check=True))(kc, tq0, scs),
                  wait=wqk, inc="scmm", inc_by=1)
                wex = [("exev", max(0, P.count("exev") - 1))] if P.count("exev") >= 1 else []
                p("tensor", (lambda kc, tq0, exs: lambda e: e.matmul(
                    exs, sb_t[:, kc * 128:(kc + 1) * 128], sbq[:, tq0:tq0 + 256],
                    start=True, stop=False, skip_group_check=True))(kc, tq0, exs),
                  wait=wex)
                p("tensor", (lambda kc, tq0, exs: lambda e: e.matmul(
                    exs, hexc[:, kc * 128:(kc + 1) * 128], exth[:, tq0:tq0 + 256],
                    start=False, stop=True, skip_group_check=True))(kc, tq0, exs),
                  inc="exmm", inc_by=1)
                SCMM, EXMM = P.count("scmm"), P.count("exmm")
                sl = (lambda pp: slice(pp * 256, (pp + 1) * 256))(pp)
                p("vector", (lambda scs, sl: lambda e: e.tensor_copy(
                    sc_sb[:, sl], scs))(scs, sl),
                  wait=[("scmm", SCMM)], inc="scev", inc_by=1)
                p("vector", (lambda exs, sl: lambda e: e.tensor_copy(
                    ex_sb[:, sl], exs))(exs, sl),
                  wait=[("exmm", EXMM)], inc="exev", inc_by=1)
                p("scalar", (lambda sl: lambda e: e.activation(
                    t_sb[:, sl], sc_sb[:, sl], AF.Tanh, scale=smallB[:, 2:3]))(sl),
                  wait=[("scev", P.count("scev")), ("smd", SMB)], inc="tnh", inc_by=1)
                p("vector", (lambda sl: lambda e: e.scalar_tensor_tensor(
                    s2_sb[:, sl], t_sb[:, sl], smallB[:, 0:1], ex_sb[:, sl],
                    ALU.mult, ALU.add))(sl),
                  wait=[("tnh", P.count("tnh")), ("exev", P.count("exev"))],
                  inc="cmbs", inc_by=1)
                p("scalar", (lambda sl: lambda e: e.activation(
                    pt_sb[:, sl], s2_sb[:, sl], AF.Exp, bias=smallB[:, 1:2]))(sl),
                  wait=[("cmbs", P.count("cmbs"))], inc="expd", inc_by=1)
                rel = nkc - kc
                mslot = 0 if kc == 0 else (rel if rel <= 3 else None)
                if mslot is not None:
                    p("vector", (lambda sl, mslot: lambda e: e.tensor_mul(
                        pt_sb[:, sl], pt_sb[:, sl], mk[:, mslot * 256:(mslot + 1) * 256]))(sl, mslot),
                      wait=[("expd", P.count("expd")), ("gw", GW)], inc="mskd", inc_by=1)
                    pvw = [("mskd", P.count("mskd"))]
                else:
                    pvw = [("expd", P.count("expd"))]
                p("tensor", (lambda kc, sl, st, sp: lambda e: e.matmul(
                    ps_pv[:], vt[:, kc * 65:(kc + 1) * 65], pt_sb[:, sl],
                    start=st, stop=sp, skip_group_check=True))(kc, sl, kc == 0, kc == nkc - 1),
                  wait=pvw, inc="pvmm" if kc == nkc - 1 else None,
                  inc_by=1 if kc == nkc - 1 else None)
            PVMM = P.count("pvmm")
            P.chain("vector", [
                lambda e: e.tensor_copy(nrow[:], ps_pv[64:65, :]),
                lambda e: e.reciprocal(nrow[:], nrow[:]),
                lambda e: e.tensor_copy(nrec[:], nrow[:]),
            ], wait=[("pvmm", PVMM)], inc="nr1", inc_by=1)
            p("tensor", lambda e: e.matmul(ps_v[0:64, 0:256] if False else ps_v.ap().bitcast(F32)[0:64, 0:256] if False else ps_v[0:64, 0:64], o64[:], nrec[:, 0:64], start=True, stop=True), inc=None, inc_by=None) if False else None
            p("tensor", lambda e: e.matmul(ps_pj[0:64, 0:256], o64[:], nrec[:],
                                           start=True, stop=True, skip_group_check=True),
              wait=[("nr1", P.count("nr1"))], inc="nbm", inc_by=1)
            p("vector", (lambda: lambda e: e.tensor_copy(po_sb[:], ps_pv[0:64, :]))(),
              wait=[("nbm", P.count("nbm"))], inc="poe", inc_by=1)
            hc, hr = h // 2, (h % 2) * 64
            p("vector", (lambda hc, hr, tq0: lambda e: e.tensor_tensor(
                ao[hr:hr + 64, hc * TC + tq0:hc * TC + tq0 + 256], po_sb[:],
                ps_pj[0:64, 0:256], ALU.mult))(hc, hr, tq0),
              wait=[("poe", P.count("poe"))], inc="nrm", inc_by=1)
    AON = P.count("nrm")

    if upto == "attn":
        tap = L.out("tap", [128, NCH * TC], BF16)
        p("sync", lambda e: e.dma_start(out=tap, in_=ao[:]),
          wait=[("nrm", AON)], inc="tp", inc_by=16)
        P.wait("sync", [("tp", 16)])
        return L

    NTB = TC // 512
    for tb in range(NTB):
        t0 = tb * 512
        w0 = (tb // 2) * 1024
        for fo in range(NCH):
            gid = tb * NCH + fo
            sl0 = (gid % 2) * 1024
            for c in range(NCH):
                p("gpsimd", (lambda c, fo, sl0: lambda e: e.dma_start(
                    out=outb[:, sl0 + c * 128:sl0 + (c + 1) * 128],
                    in_=outT[c * 128:(c + 1) * 128, fo * 128:(fo + 1) * 128]))(c, fo, sl0),
                    wait=[("ymm", max(0, P.count("ymm") - 1))] if gid >= 2 and c == 0 else [],
                    inc="gow", inc_by=16)
            GOW = P.count("gow")
            items = [(outb[:, sl0 + c * 128:sl0 + (c + 1) * 128],
                      ao[:, c * TC + t0:c * TC + t0 + 512],
                      ([("nrm", AON)] if gid == 0 and c == 0 else []) +
                      ([("gow", GOW)] if c == 0 else []))
                     for c in range(NCH)]
            mm_group(P, ps_pj[:, 0:512], items,
                     wait_first=[("yev", P.count("yev"))] if gid else [],
                     inc="ymm", inc_by=1)
            YMM = P.count("ymm")
            p("vector", (lambda fo, t0: lambda e: e.tensor_copy(
                y_sb[:, fo * TC + t0: fo * TC + t0 + 512], ps_pj[:, 0:512]))(fo, t0),
              wait=[("ymm", YMM)], inc="yev", inc_by=1)
            p("scalar", (lambda gid, fo, t0: lambda e: e.activation(
                sqb[:, (gid % 2) * 512:(gid % 2) * 512 + 512],
                y_sb[:, fo * TC + t0: fo * TC + t0 + 512], AF.Square))(gid, fo, t0),
                wait=[("yev", P.count("yev"))] + ([("ysqmm", max(0, P.count("ysqmm") - 1))] if P.count("ysqmm") >= 1 else []),
                inc="ysq", inc_by=1)
            p("tensor", (lambda gid, fo: lambda e: e.matmul(
                ln_y.ps_q[:], oc[:], sqb[:, (gid % 2) * 512:(gid % 2) * 512 + 512],
                start=(fo == 0), stop=(fo == NCH - 1), skip_group_check=True))(gid, fo),
                wait=[("ysq", P.count("ysq"))], inc="ysqmm", inc_by=1)
            p("tensor", (lambda tb, fo, t0: lambda e: e.matmul(
                ln_y.ps_s[:], oc[:], y_sb[:, fo * TC + t0: fo * TC + t0 + 512],
                start=(fo == 0), stop=(fo == NCH - 1), skip_group_check=True))(tb, fo, t0),
                wait=[("yev", P.count("yev"))], inc="ysmm", inc_by=1)
        rt0 = t0 - w0
        p("vector", (lambda rt0: lambda e: e.tensor_copy(
            ln_y.rows[0:1, rt0:rt0 + 512], ln_y.ps_s[:]))(rt0),
            wait=[("ysmm", P.count("ysmm")), ("ysqmm", P.count("ysqmm"))],
            inc="ystev", inc_by=1)
        p("vector", (lambda rt0: lambda e: e.tensor_copy(
            ln_y.rows[0:1, ln_y.cap + rt0:ln_y.cap + rt0 + 512], ln_y.ps_q[:]))(rt0),
            inc="ystev", inc_by=1)
        if tb + 1 < NTB:
            P.wait("tensor", [("ystev", P.count("ystev"))])
        if tb % 2 == 1:
            p("vector", lambda e: e.engine_nop(), wait=[("ystev", P.count("ystev"))],
              inc="lnyev", inc_by=1)
            ln_y.rowmath(P, wnt=1024)
            ln_y.bcast(P, t_off=w0, wnt=1024)
            if tb + 1 < NTB:
                P.wait("vector", [ln_y.done])

    xmyb = xnf[:, 8192:8192 + TC]
    x1st = xnf[:, 8192 + TC:8192 + 2 * TC]
    for c in range(NCH):
        p("sync", (lambda c: lambda e: e.dma_start(
            out=xmyb, in_=xmy[c * 128:(c + 1) * 128, :]))(c),
            wait=[("cmb", P.count("cmb"))] if c >= 1 else [],
            inc="ldr", inc_by=16)
        LDR = P.count("ldr")
        P.chain("vector", [
            (lambda c: lambda e: e.tensor_mul(
                x1st, y_sb[:, c * TC:(c + 1) * TC], ln_y.Rbc[:]))(c),
            lambda e: e.tensor_add(x1st, x1st, ln_y.Bbc[:]),
            (lambda LDR: lambda e: (e.wait_ge(P.sems["ldr"], LDR),
                                    e.tensor_add(x1st, x1st, xmyb))[1])(LDR),
        ], wait=[ln_y.done] + ([("xst", P.count("xst"))] if c >= 1 else []),
            inc="cmb", inc_by=1)
        p("sync", (lambda c: lambda e: e.dma_start(
            out=x1t[c * 128:(c + 1) * 128, :], in_=x1st))(c),
            wait=[("cmb", P.count("cmb"))], inc="xst", inc_by=16)
    P.wait("sync", [("xst", P.count("xst"))])
    return L





def build_mamba(upto=None):
    L = LaunchBuilder()
    P = L.P
    p = P.op
    NH = 32
    DI = 2048
    CD = 2176
    NXC = 17
    BL = 512
    NBLK = T // BL
    x2 = L.inp("x2", [D, T], F32)
    w_inT = L.inp("w_inT", [D, 4256], F32)
    convw = L.inp("convw", [128, NXC * 4], F32)
    convb = L.inp("convb", [128, NXC], F32)
    dtb = L.inp("dtb", [NH, 1], F32)
    alog = L.inp("alog", [NH, 1], F32)
    dvec = L.inp("dvec", [1, NH], F32)
    nwT = L.inp("nwT", [128, 16], F32)
    w_outT = L.inp("w_outT", [DI, D], F32)
    identf = L.inp("identf", [128, 128], F32)
    x3 = L.out("x3", [D, T], F32)

    xn = L.sb("xn", [128, NCH * T], BF16)
    y2b = L.sb("y2b", [128, NCH * BL], BF16)
    xbcs = L.sb("xbcs", [128, NXC * BL], BF16)
    xbcr = L.sb("xbcr", [128, BL], F32)
    carry = L.sb("carry", [128, NXC * 3], BF16)
    zs = L.sb("zs", [128, BL], BF16)
    yg = L.sb("yg", [128, 16 * BL], BF16)
    sqb = L.sb("sqb", [128, 1024], BF16)
    dts = L.sb("dts", [32, BL], F32)
    csv = L.sb("csv", [32, BL], F32)
    sdoS = L.sb("sdoS", [32, BL], F32)
    dtdS = L.sb("dtdS", [32, BL], F32)
    clS = L.sb("clS", [32, BL], F32)
    cdec = L.sb("cdec", [32, 8], F32)
    csT = L.sb("csT", [64, 8 * 32], F32)
    cdecB = L.sb("cdecB", [64, 256], F32)
    decT = L.sb("decT", [64, 8 * 32], F32)
    DB = L.sb("DB", [128, NH], F32)
    csR = L.sb("csR", [1, NH * 64], BF16)
    csRlo = L.sb("csRlo", [1, NH * 64], BF16)
    dtT = L.sb("dtT", [64, 8 * 32], F32)
    identfs = L.sb("identfs", [128, 128], F32)
    d2sb = L.sb("d2sb", [64, 64], F32)
    e2 = L.sb("e2", [64, 64], BF16)
    S_run = L.sb("S_run", [64, NH * 64], F32)
    S_bf = L.sb("S_bf", [64, 64], BF16)
    CB_sb = L.sb("CB_sb", [64, 8 * 64], BF16)
    Btok = L.sb("Btok", [64, 8 * 64], BF16)
    M_sb = L.sb("M_sb", [64, 64], BF16)
    MT_sb = L.sb("MT_sb", [64, 64], BF16)
    xdtT_sb = L.sb("xdtT_sb", [64, 64], BF16)
    BdT = L.sb("BdT", [64, 64], BF16)
    Cs_sb = L.sb("Cs_sb", [64, 64], BF16)
    Cb0 = L.sb("Cb0", [64, BL], BF16)
    d1p = L.sb("d1p", [64, 64], F32)
    e1 = L.sb("e1", [64, 64], BF16)
    o64 = L.sb("o64", [1, 64], BF16)
    ones64 = L.sb("ones64", [64, 64], F32)
    trineg = L.sb("trineg", [64, 64], F32)
    identb = L.sb("identb", [128, 128], BF16)
    oc = L.sb("oc", [128, 1], BF16)
    wbuf = L.sb("wbuf", [128, 8 * 128], BF16)
    w2buf = L.sb("w2buf", [128, 16 * 128], BF16)
    cwb = L.sb("cwb", [128, NXC * 4], F32)
    cbb = L.sb("cbb", [128, NXC], F32)
    dtbB = L.sb("dtbB", [32, 1], F32)
    anegB = L.sb("anegB", [32, 1], F32)
    nwB = L.sb("nwB", [128, 16], F32)
    rstdB = L.sb("rstdB", [128, BL], BF16)
    rmsrow = L.sb("rmsrow", [1, 2 * BL], F32)
    xadd = L.sb("xadd", [128, BL], F32)
    wdt32 = L.sb("wdt32", [128, NCH * 32], F32)
    wdthl = L.sb("wdthl", [128, 2 * NCH * 32], BF16)
    xn32 = L.sb("xn32", [128, BL], F32)
    xnhl = L.sb("xnhl", [128, 2 * BL], BF16)
    x3st = L.sb("x3st", [128, BL], F32)

    scrB4 = L.dram("scrB4", [NH, BL], F32)
    scrB5 = L.dram("scrB5", [NH, BL], F32)
    scrC = L.dram("scrC", [1, 256], F32)
    scrR = L.dram("scrR", [1, BL], F32)

    ps_w = L.ps("ps_w", [128, 512])
    ps_tr = L.ps("ps_tr", [128, 64], BF16)
    ps_trf = L.ps("ps_trf", [64, 32])
    ps_b1 = L.ps("ps_b1", [64, 64])
    ps_st = L.ps("ps_st", [64, 64])
    ps_y = L.ps("ps_y", [64, 64])
    csvLo = sdoS
    csvHb = dtdS.ap().bitcast(BF16)[:, 0:BL]
    ln_x = LNHelper(L, "lnx", T, rows_cap=512, bc_dtype=F32,
                    ps_override=None, tmp_override=rmsrow[0:1, BL:2 * BL])
    ln_y = LNHelper(L, "lny", BL, inv_n=1.0 / D, share=ln_x, bc_dtype=BF16,
                    bc_override=(ln_x.Rbc[:, 0:BL], ln_x.Bbc[:, 0:BL]))

    for c in range(NCH):
        p("gpsimd", (lambda c: lambda e: e.dma_start(
            out=xn[:, c * T:(c + 1) * T], in_=x2[c * 128:(c + 1) * 128, :]))(c),
            inc="gx", inc_by=16)
    p("sync", lambda e: e.dma_start(out=cwb[:], in_=convw), inc="sk", inc_by=16)
    p("sync", lambda e: e.dma_start(out=cbb[:], in_=convb), inc="sk", inc_by=16)
    p("sync", lambda e: e.dma_start(out=dtbB[:], in_=dtb), inc="sk", inc_by=16)
    p("sync", lambda e: e.dma_start(out=anegB[:], in_=alog), inc="sk", inc_by=16)
    p("sync", lambda e: e.dma_start(out=nwB[:], in_=nwT), inc="sk", inc_by=16)
    p("sync", lambda e: e.dma_start(out=DB[:], in_=dvec.broadcast_to((128, NH))),
      inc="sk", inc_by=16)
    p("gpsimd", lambda e: e.dma_start(out=identb[:], in_=identf), inc="gx", inc_by=16)
    p("sync", lambda e: e.dma_start(out=identfs[:], in_=identf), inc="sk", inc_by=16)
    GX, SK = P.count("gx"), P.count("sk")
    p("vector", lambda e: e.memset(oc[:], 1.0), inc="vi", inc_by=1)
    p("vector", lambda e: e.memset(o64[:], 1.0), inc="vi", inc_by=1)
    p("vector", lambda e: e.memset(ones64[:], 1.0), inc="vi", inc_by=1)
    p("vector", lambda e: e.memset(carry[:], 0.0), inc="vi", inc_by=1)
    p("vector", lambda e: e.memset(S_run[:], 0.0), inc="vi", inc_by=1)
    p("gpsimd", lambda e: e.iota(trineg.ap(), pattern=[[1, 64]], base=0,
                                 channel_multiplier=0), inc="gi", inc_by=1) if False else None
    trin = L.inp("trin", [64, 64], F32)
    p("sync", lambda e: e.dma_start(out=trineg[:], in_=trin), inc="sk2", inc_by=16)
    SK2 = P.count("sk2")
    p("scalar", lambda e: e.activation(anegB[:], anegB[:], AF.Exp),
      wait=[("sk", SK)], inc="an", inc_by=1)
    p("vector", lambda e: e.tensor_scalar(anegB[:], anegB[:], -1.0, None, ALU.mult),
      wait=[("an", P.count("an"))], inc="an2", inc_by=1)
    AN2 = P.count("an2")
    VI = P.count("vi")

    for c in range(NCH):
        p("sync", (lambda c: lambda e: e.dma_start(
            out=wdt32[:, c * 32:(c + 1) * 32],
            in_=w_inT[c * 128:(c + 1) * 128, 4224:4256]))(c), inc="wd", inc_by=16)
    P.chain("vector", [
        lambda e: e.tensor_copy(wdthl[:, 0:NCH * 32], wdt32[:]),
        lambda e: e.tensor_sub(wdt32[:], wdt32[:], wdthl[:, 0:NCH * 32]),
        lambda e: e.tensor_copy(wdthl[:, NCH * 32:], wdt32[:]),
    ], wait=[("wd", P.count("wd"))], inc="wds", inc_by=1)
    WDS = P.count("wds")

    def xchunk(c, t0, t1):
        return xn[:, c * T + t0: c * T + t1]
    ln_x.full(P, oc, xchunk, NCH, (sqb, "free"), [("gx", GX), ("vi", VI)])
    for c in range(NCH):
        ln_x.apply(P, xn[:, c * T:(c + 1) * T], xn[:, c * T:(c + 1) * T],
                   0, T, inc="xnp", inc_by=1)
    XN = P.count("xnp")

    for bi in range(NBLK):
        t0 = bi * BL
        for ch in range(NXC):
            fo0 = DI + ch * 128
            sl0 = 0
            for c in range(NCH):
                p("gpsimd", (lambda c, fo0, sl0: lambda e: e.dma_start(
                    out=wbuf[:, sl0 + c * 128:sl0 + (c + 1) * 128],
                    in_=w_inT[c * 128:(c + 1) * 128, fo0:fo0 + 128]))(c, fo0, sl0),
                    wait=((([("zpm", P.count("zpm"))] if (ch == 0 and P.count("zpm")) else []) +
                           ([("ipm", P.count("ipm"))] if P.count("ipm") else []))
                          if c == 0 else []),
                    inc="gwi", inc_by=16)
            GWI = P.count("gwi")
            items = [(wbuf[:, sl0 + c * 128:sl0 + (c + 1) * 128],
                      xn[:, c * T + t0:c * T + t0 + BL],
                      ([("gwi", GWI), ("xnp", XN)] if c == 0 else []))
                     for c in range(NCH)]
            mm_group(P, ps_w[:, 0:512], items,
                     wait_first=([("ope", P.count("ope"))] if (ch == 0 and P.count("ope")) else [])
                                + ([("ipe", P.count("ipe"))] if P.count("ipe") else []),
                     inc="ipm", inc_by=1)
            p("vector", (lambda: lambda e: e.tensor_copy(xbcr[:], ps_w[:, 0:512]))(),
              wait=[("ipm", P.count("ipm"))], inc="ipe", inc_by=1)
            cw = lambda k: cwb[:, ch * 4 + k:ch * 4 + k + 1]
            P.chain("vector", [
                (lambda ch: lambda e: e.tensor_scalar(
                    xn32[:], xbcr[:], cwb[:, ch * 4 + 3:ch * 4 + 4], None, ALU.mult))(ch),
                (lambda ch: lambda e: e.scalar_tensor_tensor(
                    xn32[:, 1:512], xbcr[:, 0:511], cwb[:, ch * 4 + 2:ch * 4 + 3],
                    xn32[:, 1:512], ALU.mult, ALU.add))(ch),
                (lambda ch: lambda e: e.scalar_tensor_tensor(
                    xn32[:, 0:1], carry[:, ch * 3 + 2:ch * 3 + 3], cwb[:, ch * 4 + 2:ch * 4 + 3],
                    xn32[:, 0:1], ALU.mult, ALU.add))(ch),
                (lambda ch: lambda e: e.scalar_tensor_tensor(
                    xn32[:, 2:512], xbcr[:, 0:510], cwb[:, ch * 4 + 1:ch * 4 + 2],
                    xn32[:, 2:512], ALU.mult, ALU.add))(ch),
                (lambda ch: lambda e: e.scalar_tensor_tensor(
                    xn32[:, 0:2], carry[:, ch * 3 + 1:ch * 3 + 3], cwb[:, ch * 4 + 1:ch * 4 + 2],
                    xn32[:, 0:2], ALU.mult, ALU.add))(ch),
                (lambda ch: lambda e: e.scalar_tensor_tensor(
                    xn32[:, 3:512], xbcr[:, 0:509], cwb[:, ch * 4 + 0:ch * 4 + 1],
                    xn32[:, 3:512], ALU.mult, ALU.add))(ch),
                (lambda ch: lambda e: e.scalar_tensor_tensor(
                    xn32[:, 0:3], carry[:, ch * 3:ch * 3 + 3], cwb[:, ch * 4 + 0:ch * 4 + 1],
                    xn32[:, 0:3], ALU.mult, ALU.add))(ch),
                (lambda ch: lambda e: e.tensor_scalar(
                    xn32[:], xn32[:], cbb[:, ch:ch + 1], None, ALU.add))(ch),
                (lambda ch: lambda e: e.tensor_copy(
                    carry[:, ch * 3:ch * 3 + 3], xbcr[:, 509:512]))(ch),
            ], wait=[("ipe", P.count("ipe")), ("sk", SK),
                     ("cve", P.count("cve"))], inc="cvc", inc_by=1)
            p("scalar", (lambda ch: lambda e: e.activation(
                xbcs[:, ch * BL:(ch + 1) * BL], xn32[:], AF.Silu))(ch),
              wait=[("cvc", P.count("cvc"))], inc="cve", inc_by=1)
        CVE = P.count("cve")
        for c in range(NCH):
            pp = c % 2
            p("sync", (lambda c, t0: lambda e: e.dma_start(
                out=xn32[:], in_=x2[c * 128:(c + 1) * 128, t0:t0 + BL]))(c, t0),
                wait=([("xhl", P.count("xhl"))] if P.count("xhl") else []) +
                     ([("cve", P.count("cve"))] if c == 0 else []),
                inc="x32", inc_by=16)
            X32 = P.count("x32")
            cw = [("x32", X32), ("xnp", XN), ln_x.done]
            if c >= 2:
                cw.append(("dtp", 3 * (c - 1)))
            P.chain("vector", [
                (lambda t0: lambda e: e.tensor_mul(
                    xn32[:], xn32[:], ln_x.Rbc[:, t0:t0 + BL]))(t0),
                (lambda t0: lambda e: e.tensor_add(
                    xn32[:], xn32[:], ln_x.Bbc[:, t0:t0 + BL]))(t0),
                (lambda c, t0: lambda e: e.tensor_sub(
                    xn32[:], xn32[:],
                    xn[:, c * T + t0:c * T + t0 + BL]))(c, t0),
                (lambda pp: lambda e: e.tensor_copy(
                    xnhl[:, pp * BL:(pp + 1) * BL], xn32[:]))(pp),
            ], wait=cw, inc="xhl", inc_by=1)
            XHL = P.count("xhl")
            mwait = [("xhl", XHL)]
            if c == 0:
                mwait += [("wds", WDS), ("ipe", P.count("ipe")), ("dte", P.count("dte"))]
            p("tensor", (lambda c, t0, st: lambda e: e.matmul(
                ps_w[0:32, 0:512], wdthl[:, c * 32:(c + 1) * 32],
                xn[:, c * T + t0:c * T + t0 + BL],
                start=st, stop=False, skip_group_check=True))(c, t0, c == 0),
                wait=mwait, inc="dtp", inc_by=1)
            p("tensor", (lambda c, t0: lambda e: e.matmul(
                ps_w[0:32, 0:512], wdthl[:, (NCH + c) * 32:(NCH + c + 1) * 32],
                xn[:, c * T + t0:c * T + t0 + BL],
                start=False, stop=False, skip_group_check=True))(c, t0),
                inc="dtp", inc_by=1)
            p("tensor", (lambda c, pp, sp: lambda e: e.matmul(
                ps_w[0:32, 0:512], wdthl[:, c * 32:(c + 1) * 32],
                xnhl[:, pp * BL:(pp + 1) * BL],
                start=False, stop=sp, skip_group_check=True))(c, pp, c == NCH - 1),
                inc="dtm" if c == NCH - 1 else "dtp",
                inc_by=1 if c == NCH - 1 else 1)
        P.chain("vector", [
            lambda e: e.tensor_copy(dts[:], ps_w[0:32, 0:512]),
            lambda e: e.tensor_scalar(dts[:], dts[:], dtbB[:, 0:1], None, ALU.add),
            lambda e: e.tensor_scalar(clS[:], dts[:], -1.0, None, ALU.mult),
            lambda e: e.tensor_max(clS[:], clS[:], dts[:]),
        ], wait=[("dtm", P.count("dtm")), ("an2", AN2)], inc="dte", inc_by=1)
        p("scalar", lambda e: e.activation(sdoS[:], clS[:], AF.Exp, scale=-1.0),
          wait=[("dte", P.count("dte")), ("bnc", P.count("bnc"))], inc="dts1", inc_by=1)
        P.chain("vector", [
            lambda e: e.tensor_scalar(sdoS[:], sdoS[:], 1.0, None, ALU.add),
        ], wait=[("dts1", P.count("dts1"))], inc="dts2", inc_by=1)
        p("scalar", lambda e: e.activation(sdoS[:], sdoS[:], AF.Ln),
          wait=[("dts2", P.count("dts2"))], inc="dts3", inc_by=1)
        P.chain("vector", [
            lambda e: e.tensor_scalar_max(dtdS[:], dts[:], 0.0),
            lambda e: e.tensor_add(dts[:], dtdS[:], sdoS[:]),
            lambda e: e.tensor_scalar(dtdS[:], dts[:], anegB[:, 0:1], None, ALU.mult),
        ], wait=[("dts3", P.count("dts3")), ("cshl", P.count("cshl"))], inc="dts4", inc_by=1)
        for c8 in range(8):
            p("vector", (lambda c8: lambda e: e.tensor_tensor_scan(
                csv[:, c8 * 64:(c8 + 1) * 64], dtdS[:, c8 * 64:(c8 + 1) * 64],
                dtdS[:, c8 * 64:(c8 + 1) * 64], 0.0, ALU.add, ALU.bypass))(c8),
              wait=[("dts4", P.count("dts4"))] if c8 == 0 else [], inc="css", inc_by=1)
        CSS = P.count("css")
        for c8 in range(8):
            p("vector", (lambda c8: lambda e: e.tensor_scalar(
                clS[:, c8 * 64:(c8 + 1) * 64], ones64[0:32, :],
                csv[:, c8 * 64 + 63:c8 * 64 + 64], None, ALU.mult))(c8),
              wait=[("css", CSS)] if c8 == 0 else [], inc="cl1", inc_by=1)
        P.chain("vector", [
            lambda e: e.tensor_sub(clS[:], clS[:], csv[:]),
        ], wait=[("cl1", P.count("cl1"))], inc="cl2", inc_by=1)
        p("scalar", lambda e: e.activation(clS[:], clS[:], AF.Exp),
          wait=[("cl2", P.count("cl2"))], inc="cl3", inc_by=1)
        p("scalar", lambda e: e.activation(cdec[:], csv[:, 63:512:64], AF.Exp),
          inc="cl3", inc_by=1)
        CL3 = P.count("cl3")
        P.chain("vector", [
            lambda e: e.tensor_copy(csvHb, csv[:]),
            lambda e: e.tensor_sub(csvLo[:], csv[:], csvHb),
        ], wait=[("css", CSS), ("cl1", P.count("cl1"))], inc="cshl", inc_by=1)
        p("sync", lambda e: e.dma_start(out=scrB4, in_=csv[:]),
          wait=[("css", CSS), ("csru", P.count("csru"))], inc="bnc", inc_by=16)
        p("sync", lambda e: e.dma_start(out=scrB5, in_=csvLo[:]),
          wait=[("cshl", P.count("cshl")), ("csru", P.count("csru"))], inc="bnc", inc_by=16)
        p("sync", lambda e: e.dma_start(out=scrC, in_=cdec[:]),
          wait=[("cl3", CL3)], inc="bnc", inc_by=16)

        p("sync", lambda e: e.dma_start(out=cdecB[:], in_=scrC.broadcast_to((64, 256))),
          wait=[("bnc", P.count("bnc"))], inc="bncs", inc_by=16)
        BNCG, BNCS = P.count("bncg"), P.count("bncs")
        for c8 in range(8):
            for src, dst in ((csv, csT), (clS, decT), (dts, dtT)):
                p("tensor", (lambda c8, src: lambda e: e.transpose(
                    ps_trf[:], src[:, c8 * 64:(c8 + 1) * 64], identfs[0:32, 0:32]))(c8, src),
                  wait=([("cl3", CL3), ("sk", SK)] if (c8 == 0 and src is csv) else [])
                       + [("trc", P.count("trc"))],
                  inc="trm", inc_by=1)
                p("vector", (lambda c8, dst: lambda e: e.tensor_copy(
                    dst[:, c8 * 32:(c8 + 1) * 32], ps_trf[:]))(c8, dst),
                  wait=[("trm", P.count("trm"))], inc="trc", inc_by=1)
        TRC = P.count("trc")
        p("vector", lambda e: e.tensor_copy(Cb0[:], xbcs[64:128, 16 * BL:17 * BL]),
          wait=[("cve", CVE)] + ([("ssd4", P.count("ssd4"))] if P.count("ssd4") else []),
          inc="cb0", inc_by=1)
        CB0 = P.count("cb0")
        for c8 in range(8):
            p("tensor", (lambda c8: lambda e: e.transpose(
                ps_tr[0:64, 0:64], xbcs[0:64, 16 * BL + c8 * 64:16 * BL + (c8 + 1) * 64],
                identb[0:64, 0:64]))(c8),
              wait=([("cve", CVE)] if c8 == 0 else []) + [("btc", P.count("btc"))],
              inc="btm", inc_by=1)
            p("vector", (lambda c8: lambda e: e.tensor_copy(
                Btok[:, c8 * 64:(c8 + 1) * 64], ps_tr[0:64, 0:64]))(c8),
              wait=[("btm", P.count("btm"))], inc="btc", inc_by=1)
            p("tensor", (lambda c8: lambda e: e.matmul(
                ps_st[:], Cb0[:, c8 * 64:(c8 + 1) * 64],
                xbcs[0:64, 16 * BL + c8 * 64:16 * BL + (c8 + 1) * 64],
                start=True, stop=True))(c8),
              wait=([("cbc", P.count("cbc"))] if c8 else [("cb0", CB0)]),
              inc="cbm", inc_by=1)
            p("vector", (lambda c8: lambda e: e.tensor_copy(
                CB_sb[:, c8 * 64:(c8 + 1) * 64], ps_st[:]))(c8),
              wait=[("cbm", P.count("cbm"))], inc="cbc", inc_by=1)
        BTC, CBC = P.count("btc"), P.count("cbc")

        for c8 in range(8):
            csl = 0
            p("gpsimd", (lambda c8, csl: lambda e: e.dma_start(
                out=csR[0:1, csl:csl + 2048].rearrange("a (h s) -> a h s", s=64),
                in_=scrB4[:, c8 * 64:(c8 + 1) * 64].unsqueeze(0)))(c8, csl),
                wait=[("bnc", P.count("bnc")), ("ssb2", P.count("ssb2"))],
                inc="bncg", inc_by=16)
            p("gpsimd", (lambda c8, csl: lambda e: e.dma_start(
                out=csRlo[0:1, csl:csl + 2048].rearrange("a (h s) -> a h s", s=64),
                in_=scrB5[:, c8 * 64:(c8 + 1) * 64].unsqueeze(0)))(c8, csl),
                inc="bncg", inc_by=16)
            BNCG = P.count("bncg")
            for h in range(NH):
                hc, hr = h // 2, (h % 2) * 64
                xtile = xbcs[hr:hr + 64, hc * BL + c8 * 64:hc * BL + (c8 + 1) * 64]
                p("tensor", (lambda xtile, hr: lambda e: e.transpose(
                    ps_tr[0:64, 0:64], xtile, identb[hr:hr + 64, hr:hr + 64]))(xtile, hr),
                  wait=([("btc", BTC), ("bncg", BNCG)] if (c8 == 0 and h == 0) else [])
                       + [("sse1", P.count("sse1"))],
                  inc="sst1", inc_by=1)
                p("vector", (lambda h, c8: lambda e: e.tensor_scalar(
                    xdtT_sb[:], ps_tr[0:64, 0:64],
                    dtT[:, c8 * 32 + h:c8 * 32 + h + 1], None, ALU.mult))(h, c8),
                  wait=[("sst1", P.count("sst1")), ("trc", TRC)], inc="sse1", inc_by=1)
                p("tensor", (lambda h, csl: lambda e: e.matmul(
                    ps_b1[:], o64[:], csR[0:1, csl + h * 64:csl + (h + 1) * 64],
                    start=True, stop=False, skip_group_check=True))(h, csl),
                  wait=[("ssd2", P.count("ssd2")), ("bncg", BNCG)],
                  inc="ssb2", inc_by=1)
                p("tensor", (lambda h, csl: lambda e: e.matmul(
                    ps_b1[:], o64[:], csRlo[0:1, csl + h * 64:csl + (h + 1) * 64],
                    start=False, stop=True, skip_group_check=True))(h, csl),
                  inc="ssb2", inc_by=1)
                P.chain("vector", [
                    (lambda h, c8: lambda e: e.scalar_tensor_tensor(
                        d1p[:], ones64[:], csT[:, c8 * 32 + h:c8 * 32 + h + 1],
                        trineg[:], ALU.mult, ALU.add))(h, c8),
                    lambda e: e.tensor_copy(d2sb[:], ps_b1[:]),
                    lambda e: e.tensor_sub(d1p[:], d1p[:], d2sb[:]),
                ], wait=[("ssb2", P.count("ssb2"))], inc="ssd2", inc_by=1)
                p("scalar", (lambda: lambda e: e.activation(e1[:], d1p[:], AF.Exp))(),
                  wait=[("ssd2", P.count("ssd2"))], inc="ssx1", inc_by=1)
                p("scalar", (lambda: lambda e: e.activation(e2[:], d2sb[:], AF.Exp))(),
                  inc="ssx1", inc_by=1)
                p("vector", (lambda c8: lambda e: e.tensor_mul(
                    M_sb[:], e1[:], CB_sb[:, c8 * 64:(c8 + 1) * 64]))(c8),
                  wait=[("ssx1", P.count("ssx1")), ("cbc", CBC)], inc="ssd3", inc_by=1)
                p("tensor", (lambda: lambda e: e.transpose(
                    ps_tr[0:64, 0:64], M_sb[:], identb[0:64, 0:64]))(),
                  wait=[("ssd3", P.count("ssd3"))], inc="sst2", inc_by=1)
                p("vector", lambda e: e.tensor_copy(MT_sb[:], ps_tr[0:64, 0:64]),
                  wait=[("sst2", P.count("sst2"))], inc="sse2", inc_by=1)
                P.chain("vector", [
                    (lambda h, c8: lambda e: e.tensor_scalar(
                        BdT[:], Btok[:, c8 * 64:(c8 + 1) * 64],
                        decT[:, c8 * 32 + h:c8 * 32 + h + 1], None, ALU.mult))(h, c8),
                    (lambda c8: lambda e: e.tensor_mul(
                        Cs_sb[:], Cb0[:, c8 * 64:(c8 + 1) * 64], e2[:]))(c8),
                    (lambda h: lambda e: e.tensor_copy(
                        S_bf[:], S_run[:, h * 64:(h + 1) * 64]))(h),
                ], wait=[("ssx1", P.count("ssx1")), ("btc", BTC), ("trc", TRC)],
                    inc="ssd4", inc_by=1)
                p("tensor", (lambda: lambda e: e.matmul(
                    ps_y[:], xdtT_sb[:], MT_sb[:], start=True, stop=False,
                    skip_group_check=True))(),
                  wait=[("sse2", P.count("sse2")), ("ssd4", P.count("ssd4")),
                        ("ygw", P.count("ygw"))],
                  inc=None, inc_by=None)
                p("tensor", (lambda: lambda e: e.matmul(
                    ps_y[:], S_bf[:], Cs_sb[:], start=False, stop=True,
                    skip_group_check=True))(),
                  inc="ssym", inc_by=1)
                p("tensor", (lambda: lambda e: e.matmul(
                    ps_st[:], BdT[:], xdtT_sb[:], start=True, stop=True,
                    skip_group_check=True))(),
                  wait=[("scan", P.count("scan"))] +
                       ([("cbc", CBC)] if (c8 == 0 and h == 0) else []),
                  inc="sstm", inc_by=1)
                p("vector", (lambda h, c8, hc, hr, xtile: lambda e: e.scalar_tensor_tensor(
                    yg[hr:hr + 64, hc * BL + c8 * 64:hc * BL + (c8 + 1) * 64],
                    xtile, DB[hr:hr + 64, h:h + 1], ps_y[:], ALU.mult, ALU.add))(h, c8, hc, hr, xtile),
                  wait=[("ssym", P.count("ssym")), ("sk", SK)], inc="ygw", inc_by=1)
                p("vector", (lambda h, c8: lambda e: e.scalar_tensor_tensor(
                    S_run[:, h * 64:(h + 1) * 64], S_run[:, h * 64:(h + 1) * 64],
                    cdecB[:, h * 8 + c8:h * 8 + c8 + 1], ps_st[:],
                    ALU.mult, ALU.add))(h, c8),
                  wait=[("sstm", P.count("sstm")), ("bncs", BNCS)], inc="scan", inc_by=1)
        YGW = P.count("ygw")
        SCAN = P.count("scan")
        if upto == "blk0" and bi == 0:
            tap1 = L.out("tap1", [128, NXC * BL], BF16)
            tap2 = L.out("tap2", [32, 2 * BL], F32)
            tap3 = L.out("tap3", [128, 16 * BL], BF16)
            p("sync", lambda e: e.dma_start(out=tap1, in_=xbcs[:]),
              wait=[("cve", CVE)], inc="tp", inc_by=16)
            p("sync", lambda e: e.dma_start(out=tap2[:, 0:BL], in_=dts[:]),
              wait=[("css", CSS)], inc="tp", inc_by=16)
            p("sync", lambda e: e.dma_start(out=tap2[:, BL:], in_=csv[:]),
              inc="tp", inc_by=16)
            p("sync", lambda e: e.dma_start(out=tap3, in_=yg[:]),
              wait=[("ygw", YGW)], inc="tp", inc_by=16)
            P.wait("sync", [("tp", P.count("tp"))])
            return L
        p("tensor", lambda e: e.matmul(ps_b1[0:1, 0:1], o64[0:1, 0:1], o64[0:1, 0:1],
                                       start=True, stop=True, skip_group_check=True),
          wait=[("ssb2", P.count("ssb2"))], inc="csru", inc_by=1)

        for ch in range(16):
            sl0 = 0
            for c in range(NCH):
                p("gpsimd", (lambda c, ch, sl0: lambda e: e.dma_start(
                    out=wbuf[:, sl0 + c * 128:sl0 + (c + 1) * 128],
                    in_=w_inT[c * 128:(c + 1) * 128, ch * 128:(ch + 1) * 128]))(c, ch, sl0),
                    wait=((([("zpm", P.count("zpm"))] if P.count("zpm") else []) +
                           ([("dtm", P.count("dtm"))] if ch == 0 else []))
                          if c == 0 else []),
                    inc="gwz", inc_by=16)
            GWZ = P.count("gwz")
            items = [(wbuf[:, sl0 + c * 128:sl0 + (c + 1) * 128],
                      xn[:, c * T + t0:c * T + t0 + BL],
                      ([("gwz", GWZ)] if c == 0 else [])) for c in range(NCH)]
            mm_group(P, ps_w[:, 0:512], items,
                     wait_first=([("zpe", P.count("zpe"))] if P.count("zpe") else
                                 [("ipe", P.count("ipe"))]) + [("dte", P.count("dte"))],
                     inc="zpm", inc_by=1)
            p("vector", (lambda ch: lambda e: e.tensor_copy(
                zs[:], ps_w[:, 0:512]))(ch),
              wait=[("zpm", P.count("zpm"))] +
                   ([("ygg", P.count("ygg"))] if P.count("ygg") else []),
              inc="zpe", inc_by=1)
            p("scalar", (lambda ch: lambda e: e.activation(
                zs[:], zs[:], AF.Silu))(ch),
              wait=[("zpe", P.count("zpe"))], inc="zps", inc_by=1)
            p("vector", (lambda ch: lambda e: e.tensor_mul(
                yg[:, ch * BL:(ch + 1) * BL], yg[:, ch * BL:(ch + 1) * BL],
                zs[:]))(ch),
              wait=[("zps", P.count("zps")), ("ygw", YGW)], inc="ygg", inc_by=1)
            p("scalar", (lambda ch: lambda e: e.activation(
                sqb[:, (ch % 2) * 512:(ch % 2) * 512 + 512],
                yg[:, ch * BL:(ch + 1) * BL], AF.Square))(ch),
              wait=[("ygg", P.count("ygg"))] +
                   ([("rsm", max(0, P.count("rsm") - 1))] if P.count("rsm") >= 1 else []),
              inc="rsq", inc_by=1)
            p("tensor", (lambda ch: lambda e: e.matmul(
                ln_x.ps_s[:], oc[:], sqb[:, (ch % 2) * 512:(ch % 2) * 512 + 512],
                start=(ch == 0), stop=(ch == 15), skip_group_check=True))(ch),
              wait=[("rsq", P.count("rsq"))], inc="rsm", inc_by=1)
        P.chain("vector", [
            lambda e: e.tensor_copy(rmsrow[0:1, 0:BL], ln_x.ps_s[:]),
            lambda e: e.tensor_scalar(rmsrow[0:1, 0:BL], rmsrow[0:1, 0:BL],
                                      1.0 / DI, EPS, ALU.mult, ALU.add),
        ], wait=[("rsm", P.count("rsm"))], inc="rme", inc_by=1)
        p("scalar", lambda e: e.activation(rmsrow[0:1, BL:2 * BL], rmsrow[0:1, 0:BL], AF.Sqrt),
          wait=[("rme", P.count("rme"))], inc="rms1", inc_by=1)
        p("vector", lambda e: e.reciprocal(rmsrow[0:1, BL:2 * BL], rmsrow[0:1, BL:2 * BL]),
          wait=[("rms1", P.count("rms1"))], inc="rme2", inc_by=1)
        p("sync", lambda e: e.dma_start(out=scrR, in_=rmsrow[0:1, BL:2 * BL]),
          wait=[("rme2", P.count("rme2"))], inc="rmd", inc_by=16)
        p("gpsimd", lambda e: e.dma_start(out=rstdB[:], in_=scrR.broadcast_to((128, BL))),
          wait=[("rmd", P.count("rmd"))], inc="rmg", inc_by=16)
        RMD = P.count("rmg")
        for ch in range(16):
            p("vector", (lambda ch: lambda e: e.scalar_tensor_tensor(
                yg[:, ch * BL:(ch + 1) * BL], yg[:, ch * BL:(ch + 1) * BL],
                nwB[:, ch:ch + 1], rstdB[:], ALU.mult, ALU.mult))(ch),
              wait=[("rmg", RMD)] if ch == 0 else [], inc="ynw", inc_by=1)
        YNW = P.count("ynw")
        for fo in range(NCH):
            sl0 = 0
            for c in range(16):
                p("gpsimd", (lambda c, fo, sl0: lambda e: e.dma_start(
                    out=w2buf[:, sl0 + c * 128:sl0 + (c + 1) * 128],
                    in_=w_outT[c * 128:(c + 1) * 128, fo * 128:(fo + 1) * 128]))(c, fo, sl0),
                    wait=[("opm", P.count("opm"))] if (P.count("opm") and c == 0) else [],
                    inc="gwo", inc_by=16)
            GWO = P.count("gwo")
            items = [(w2buf[:, sl0 + c * 128:sl0 + (c + 1) * 128],
                      yg[:, c * BL:(c + 1) * BL],
                      ([("gwo", GWO), ("ynw", YNW)] if c == 0 else [])) for c in range(16)]
            mm_group(P, ps_w[:, 0:512], items,
                     wait_first=[("ope", P.count("ope"))] if P.count("ope") else
                                [("zpe", P.count("zpe"))],
                     inc="opm", inc_by=1)
            p("vector", (lambda fo: lambda e: e.tensor_copy(
                y2b[:, fo * BL:(fo + 1) * BL], ps_w[:, 0:512]))(fo),
              wait=[("opm", P.count("opm"))], inc="ope", inc_by=1)
        OPE = P.count("ope")
        for c in range(NCH):
            p("tensor", (lambda c, st, sp: lambda e: e.matmul(
                ln_x.ps_s[:], oc[:], y2b[:, c * BL:(c + 1) * BL],
                start=st, stop=sp, skip_group_check=True))(c, c == 0, c == NCH - 1),
                wait=([("ope", OPE)] if c == 0 else []) +
                     ([("lns", P.count("lns"))] if c == 0 and P.count("lns") else []),
                inc="lnm" if c == NCH - 1 else None,
                inc_by=1 if c == NCH - 1 else None)
        P.chain("vector", [
            lambda e: e.tensor_copy(rmsrow[0:1, 0:BL], ln_x.ps_s[:]),
            lambda e: e.tensor_scalar(rmsrow[0:1, 0:BL],
                                      rmsrow[0:1, 0:BL], 1.0 / D, None, ALU.mult),
        ], wait=[("lnm", P.count("lnm"))], inc="lne", inc_by=1)
        p("sync", lambda e: e.dma_start(out=scrR, in_=rmsrow[0:1, 0:BL]),
          wait=[("lne", P.count("lne"))], inc="lnd", inc_by=16)
        p("gpsimd", lambda e: e.dma_start(out=rstdB[:], in_=scrR.broadcast_to((128, BL))),
          wait=[("lnd", P.count("lnd"))], inc="lng", inc_by=16)
        LND = P.count("lng")
        for c in range(NCH):
            p("vector", (lambda c: lambda e: e.tensor_sub(
                y2b[:, c * BL:(c + 1) * BL], y2b[:, c * BL:(c + 1) * BL], rstdB[:]))(c),
              wait=[("lng", LND)] if c == 0 else [], inc="lnc", inc_by=1)
        LNC = P.count("lnc")
        for c in range(NCH):
            p("scalar", (lambda c: lambda e: e.activation(
                sqb[:, (c % 2) * 512:(c % 2) * 512 + 512],
                y2b[:, c * BL:(c + 1) * BL], AF.Square))(c),
              wait=[("lnc", LNC)] +
                   ([("lnq", max(0, P.count("lnq") - 1))] if P.count("lnq") >= 1 else []),
              inc="lsq", inc_by=1)
            p("tensor", (lambda c, st, sp: lambda e: e.matmul(
                ln_x.ps_q[:], oc[:], sqb[:, (c % 2) * 512:(c % 2) * 512 + 512],
                start=st, stop=sp, skip_group_check=True))(c, c == 0, c == NCH - 1),
                wait=[("lsq", P.count("lsq"))], inc="lnq", inc_by=1)
        P.chain("vector", [
            lambda e: e.tensor_copy(rmsrow[0:1, 0:BL], ln_x.ps_q[:]),
            lambda e: e.tensor_scalar(rmsrow[0:1, 0:BL],
                                      rmsrow[0:1, 0:BL], 1.0 / D, EPS,
                                      ALU.mult, ALU.add),
        ], wait=[("lnq", P.count("lnq"))], inc="lne2", inc_by=1)
        p("scalar", lambda e: e.activation(rmsrow[0:1, 0:BL],
                                           rmsrow[0:1, 0:BL], AF.Sqrt),
          wait=[("lne2", P.count("lne2"))], inc="lne3", inc_by=1)
        p("vector", lambda e: e.reciprocal(rmsrow[0:1, 0:BL],
                                           rmsrow[0:1, 0:BL]),
          wait=[("lne3", P.count("lne3"))], inc="lne4", inc_by=1)
        p("sync", lambda e: e.dma_start(out=scrR, in_=rmsrow[0:1, 0:BL]),
          wait=[("lne4", P.count("lne4"))], inc="lnd2", inc_by=16)
        p("gpsimd", lambda e: e.dma_start(out=rstdB[:], in_=scrR.broadcast_to((128, BL))),
          wait=[("lnd2", P.count("lnd2"))], inc="lng2", inc_by=16)
        LND2 = P.count("lng2")
        for c in range(NCH):
            p("sync", (lambda c, t0: lambda e: e.dma_start(
                out=xadd[:], in_=x2[c * 128:(c + 1) * 128, t0:t0 + BL]))(c, t0),
                wait=[("cmb", P.count("cmb"))] if P.count("cmb") else [],
                inc="ldr", inc_by=16)
            LDR = P.count("ldr")
            P.chain("vector", [
                (lambda c: lambda e: e.tensor_mul(
                    x3st[:], y2b[:, c * BL:(c + 1) * BL], rstdB[:]))(c),
                (lambda LDR: lambda e: (e.wait_ge(P.sems["ldr"], LDR),
                                        e.tensor_add(x3st[:], x3st[:], xadd[:]))[1])(LDR),
            ], wait=[("lng2", LND2)] + ([("xst", P.count("xst"))] if P.count("xst") else []),
                inc="cmb", inc_by=1)
            p("sync", (lambda c, t0: lambda e: e.dma_start(
                out=x3[c * 128:(c + 1) * 128, t0:t0 + BL], in_=x3st[:]))(c, t0),
                wait=[("cmb", P.count("cmb"))], inc="xst", inc_by=16)
        p("tensor", lambda e: e.matmul(ln_x.ps_s[0:1, 0:1], oc[0:1, 0:1], oc[0:1, 0:1],
                                       start=True, stop=True, skip_group_check=True),
          wait=[("lne", P.count("lne")), ("lne2", P.count("lne2"))], inc="lns", inc_by=1)

    P.wait("sync", [("xst", P.count("xst"))])
    return L


NQT = 17
QO = 384
KO = 128
NQ = NQT * 128
NK = 2560 - KO
NJ = 22
VW = 1040
INVSQRT2 = 0.7071067811865476


def _l1_kcol(tq):
    if tq == 0:
        return 0
    b = (3 + tq) // 2
    return (b - 1) * 256 - 128


def build_l1ffn(upto=None, ntq=NQT, nhh=16):
    L = LaunchBuilder()
    P = L.P
    p = P.op
    xin = L.inp("xin", [D, 2560], F32)
    wqkvp = L.inp("wqkvp", [128, 8 * 3072], F32)
    wop = L.inp("wop", [128, 8 * 1024], F32)
    w1p = L.inp("w1p", [128, 8 * 5460], F32)
    w2pk = L.inp("w2pk", [128, 22 * 1024], F32)
    b1a = L.inp("b1a", [128, NJ], F32)
    b1g = L.inp("b1g", [128, NJ], F32)
    b2c = L.inp("b2c", [128, NCH], F32)
    gcol = L.inp("gcol", [128, NCH], F32)
    bcol = L.inp("bcol", [128, NCH], F32)
    qsc = L.inp("qsc", [64, 1], F32)
    ksc = L.inp("ksc", [64, 1], F32)
    tabs = L.inp("tabs", [64, 4 * 2560], F32)
    masksL = L.inp("masksL", [128, 4 * 512], F32)
    identf = L.inp("identf", [128, 128], F32)
    xo = L.out("xo", [D, TC], F32)
    if upto in ("attn", "both"):
        xp_t = L.out("xp_t", [D, NQ], F32)
    if upto == "qk":
        qr_t = L.out("qr_t", [128, NCH * NQ], BF16)
        kr_t = L.out("kr_t", [128, NCH * NK], BF16)
        vt_t = L.out("vt_t", [19 * 128, VW], BF16)
    if upto == "sc":
        ao_t = L.out("ao_t", [D, NQ], BF16)

    xbL = L.sb("xbL", [128, NCH * 2560], BF16)
    BIGB = L.sb("BIGB", [128, 36864], BF16)
    QRO, KRO = 0, NCH * NQ
    FXN, FXS, FH = 0, 4096, 8192
    xosf = BIGB.ap().bitcast(F32)
    vt = L.sb("vt", [128, 6 * VW], BF16)
    tab = L.sb("tab", [64, 4 * 2560], BF16)
    mks = L.sb("mks", [128, 4 * 512], BF16)
    wbuf = L.sb("wbufL", [128, 2 * 2816], BF16)
    xr = L.sb("xrL", [128, NCH * 520], F32)
    sc_sb = L.sb("sc_sbL", [128, 512], F32)
    qf = L.sb("qfL", [128, 512], F32)
    qn = L.sb("qnL", [128, 512], F32)
    qns = L.sb("qnsL", [128, 512], F32)
    qsqb = L.sb("qsqbL", [128, 512], BF16)
    qhf = L.sb("qhfL", [64, 512], F32)
    qhn = L.sb("qhnL", [64, 512], F32)
    qhs = L.sb("qhsL", [64, 512], F32)
    pt_sb = L.sb("pt_sbL", [128, 512], BF16)
    ptT = L.sb("ptTL", [128, 512], BF16)
    pv_sb = L.sb("pv_sbL", [64, 128], F32)
    aot = L.sb("aotL", [128, NCH * 128], BF16)
    rows3 = L.sb("rows3L", [1, 512], F32)
    nr2 = L.sb("nr2L", [2, 512], F32)
    rrb = L.sb("rrbL", [1, 1024], BF16)
    srb = L.sb("srbL", [1, 128], BF16)
    srf = L.sb("srfL", [1, 128], F32)
    Rbc = L.sb("RbcL", [128, 512], F32)
    Bbc = L.sb("BbcL", [128, 512], F32)
    mxc = L.sb("mxcL", [128, 2], F32)
    onesb = L.sb("onesbL", [128, 1], BF16)
    ones1b = L.sb("ones1bL", [1, 64], BF16)
    ones2b = L.sb("ones2bL", [128, 2], BF16)
    identb = L.sb("identbL", [128, 128], BF16)
    b1ab = L.sb("b1abL", [128, NJ], F32)
    b1gb = L.sb("b1gbL", [128, NJ], F32)
    b2b = L.sb("b2bL", [128, NCH], F32)
    gbL = L.sb("gbL", [128, NCH], F32)
    bbL = L.sb("bbL", [128, NCH], F32)
    qsb = L.sb("qsbL", [64, 2], F32)

    xp_d = L.dram("xp_d", [D, NQ], F32)
    ao_d = L.dram("ao_d", [D, NQ], BF16)
    vt_d = L.dram("vt_d", [19 * 128, VW], BF16)
    scr_ln = L.dram("scr_lnL", [1, 1024], F32)
    scr_q = L.dram("scr_qL", [2, 512], F32)

    ps_mm0 = L.ps("ps_mm0L", [128, 512])
    ps_mm1 = L.ps("ps_mm1L", [128, 512])
    ps_sc = L.ps("ps_scL", [128, 512])
    ps_pv = L.ps("ps_pvL", [128, 512])
    ps_tr = L.ps("ps_trL", [128, 128], BF16)
    ps_bc1 = L.ps("ps_bc1L", [64, 512])
    ps_bc2 = L.ps("ps_bc2L", [64, 512])
    ps_s2 = L.ps("ps_s2L", [2, 512])
    PSMM = (ps_mm0, ps_mm1)

    p("gpsimd", lambda e: e.dma_start(out=tab[:], in_=tabs), inc="cst", inc_by=16)
    p("gpsimd", lambda e: e.dma_start(out=mks[:], in_=masksL), inc="cst", inc_by=16)
    p("gpsimd", lambda e: e.dma_start(out=identb[:], in_=identf), inc="cst", inc_by=16)
    p("sync", lambda e: e.dma_start(out=b1ab[:], in_=b1a), inc="csf", inc_by=16)
    p("sync", lambda e: e.dma_start(out=b1gb[:], in_=b1g), inc="csf", inc_by=16)
    p("sync", lambda e: e.dma_start(out=b2b[:], in_=b2c), inc="csf", inc_by=16)
    p("sync", lambda e: e.dma_start(out=gbL[:], in_=gcol), inc="csf", inc_by=16)
    p("sync", lambda e: e.dma_start(out=bbL[:], in_=bcol), inc="csf", inc_by=16)
    p("sync", lambda e: e.dma_start(out=qsb[:, 0:1], in_=qsc), inc="csf", inc_by=16)
    p("sync", lambda e: e.dma_start(out=qsb[:, 1:2], in_=ksc), inc="csf", inc_by=16)
    P.chain("vector", [
        lambda e: e.memset(onesb[:], 1.0),
        lambda e: e.memset(ones1b[:], 1.0),
        lambda e: e.memset(ones2b[0:64, 0:1], 1.0),
        lambda e: e.memset(ones2b[64:128, 0:1], 0.0),
        lambda e: e.memset(ones2b[0:64, 1:2], 0.0),
        lambda e: e.memset(ones2b[64:128, 1:2], 1.0),
    ], inc="cmi", inc_by=1)
    CST = P.count("cst")
    CSF = P.count("csf")
    p("vector", lambda e: e.tensor_scalar(
        qsb[:, 0:1], qsb[:, 0:1], 8.0, None, ALU.mult),
      wait=[("csf", CSF)], inc="cmi", inc_by=1)
    CMI = P.count("cmi")
    for c in range(NCH):
        p("gpsimd", (lambda c: lambda e: e.dma_start(
            out=xbL[:, c * 2560:(c + 1) * 2560],
            in_=xin[c * 128:(c + 1) * 128, :]))(c), inc="xbl", inc_by=16)
    XBL = P.count("xbl")

    def xbs(c, t0, t1):
        off = c * 2560 + (t0 if c < 4 else t0 - 1)
        return xbL[:, off:off + (t1 - t0)]

    def qk_phase(which):
        base = QO if which == "q" else KO
        n = NQ if which == "q" else NK
        reg = QRO if which == "q" else KRO
        regw = NQ if which == "q" else NK
        tabo = 0 if which == "q" else 2 * 2560
        scol = qsb[:, 0:1] if which == "q" else qsb[:, 1:2]
        ntile = cdiv(n, 512)
        for c in range(NCH):
            wslot = (c % 2) * 2816
            wof = c * 128 + (0 if which == "q" else 1024)
            pw = P.count("qkm")
            p("gpsimd", (lambda wslot, wof: lambda e: e.dma_start(
                out=wbuf[:, wslot:wslot + 1024].rearrange("p (k m) -> p k m", m=128),
                in_=wqkvp.rearrange("p (k m) -> p k m", m=3072)[:, :, wof:wof + 128]))(wslot, wof),
                wait=[("qkm", max(0, pw - ntile))], inc="wql", inc_by=16)
            WQL = P.count("wql")
            for tt in range(ntile):
                t0 = base + tt * 512
                w = min(512, base + n - t0)
                ps = PSMM[(c * ntile + tt) % 2]
                pe = P.count("qev")
                items = [(wbuf[:, wslot + k * 128:wslot + (k + 1) * 128],
                          xbs(k, t0, t0 + w), []) for k in range(NCH)]
                mm_group(P, ps[:, :w], items,
                         wait_first=[("wql", WQL), ("xbl", XBL)] +
                                    ([("qev", pe), ("qrd", P.count("qrd"))] if pe >= 1 else []),
                         inc="qkm", inc_by=1)
                QKM = P.count("qkm")
                P.chain("vector", [
                    (lambda ps, w: lambda e: e.tensor_copy(qf[0:64, :w], ps[0:64, :w]))(ps, w),
                    (lambda ps, w: lambda e: e.tensor_copy(qhf[:, :w], ps[64:128, :w]))(ps, w),
                ], wait=[("qkm", QKM), ("sqd", P.count("sqd")),
                         ("qrd", P.count("qrd"))], inc="qev", inc_by=1)
                P.chain("scalar", [
                    (lambda w: lambda e: e.activation(
                        qsqb[0:64, :w], qf[0:64, :w], AF.Square))(w),
                    (lambda w: lambda e: e.activation(
                        qsqb[64:128, :w], qhf[:, :w], AF.Square))(w),
                ], wait=[("qev", P.count("qev")), ("ssm", P.count("ssm"))],
                  inc="sqd", inc_by=1)
                p("tensor", (lambda w: lambda e: e.matmul(
                    ps_s2[:, :w], ones2b[:], qsqb[:, :w], start=True, stop=True,
                    skip_group_check=True))(w),
                  wait=[("sqd", P.count("sqd")), ("cmi", CMI),
                        ("nrc", P.count("nrc"))],
                  inc="ssm", inc_by=1)
                P.chain("vector", [
                    (lambda w: lambda e: e.tensor_copy(nr2[:, :w], ps_s2[:, :w]))(w),
                ], wait=[("ssm", P.count("ssm")), ("rrc", P.count("rrc")),
                         ("rrd", P.count("rrd"))],
                   inc="nrc", inc_by=1)
                p("scalar", (lambda w: lambda e: e.activation(
                    nr2[:, :w], nr2[:, :w], AF.Sqrt))(w),
                  wait=[("nrc", P.count("nrc"))], inc="nrs", inc_by=1)
                P.chain("vector", [
                    (lambda w: lambda e: e.tensor_scalar(
                        nr2[:, :w], nr2[:, :w], 1e-12, None, ALU.max))(w),
                    (lambda w: lambda e: e.reciprocal(nr2[:, :w], nr2[:, :w]))(w),
                ], wait=[("nrs", P.count("nrs"))], inc="rrc", inc_by=1)
                p("sync", (lambda w: lambda e: e.dma_start(
                    out=scr_q[:, :w], in_=nr2[:, :w]))(w),
                  wait=[("rrc", P.count("rrc")), ("rrg", P.count("rrg"))],
                  inc="rrd", inc_by=16)
                p("gpsimd", (lambda w: lambda e: e.dma_start(
                    out=rrb[0:1, 0:w], in_=scr_q[0:1, :w]))(w),
                  wait=[("rrd", P.count("rrd")), ("bcm", P.count("bcm"))],
                  inc="rrg", inc_by=16)
                p("gpsimd", (lambda w: lambda e: e.dma_start(
                    out=rrb[0:1, 512:512 + w], in_=scr_q[1:2, :w]))(w),
                  inc="rrg", inc_by=16)
                RRC = P.count("rrg")
                pb = P.count("qnm")
                p("tensor", (lambda w: lambda e: e.matmul(
                    ps_bc1[:, :w], ones1b[:], rrb[0:1, 0:w],
                    start=True, stop=True, skip_group_check=True))(w),
                  wait=[("rrg", RRC), ("csf", CSF)] +
                       ([("qnm", pb)] if pb >= 1 else []),
                  inc="bcm", inc_by=1)
                p("tensor", (lambda w: lambda e: e.matmul(
                    ps_bc2[:, :w], ones1b[:], rrb[0:1, 512:512 + w],
                    start=True, stop=True, skip_group_check=True))(w),
                  inc="bcm", inc_by=1)
                BCM = P.count("bcm")
                P.chain("vector", [
                    (lambda w, scol: lambda e: e.scalar_tensor_tensor(
                        qn[0:64, :w], qf[0:64, :w], scol, ps_bc1[:, :w],
                        ALU.mult, ALU.mult))(w, scol),
                    (lambda w, scol: lambda e: e.scalar_tensor_tensor(
                        qhn[:, :w], qhf[:, :w], scol, ps_bc2[:, :w],
                        ALU.mult, ALU.mult))(w, scol),
                    (lambda w: lambda e: e.tensor_copy(qns[0:32, :w], qn[32:64, :w]))(w),
                    (lambda w: lambda e: e.tensor_copy(qns[32:64, :w], qn[0:32, :w]))(w),
                    (lambda w: lambda e: e.tensor_copy(qhs[0:32, :w], qhn[32:64, :w]))(w),
                    (lambda w: lambda e: e.tensor_copy(qhs[32:64, :w], qhn[0:32, :w]))(w),
                ], wait=[("bcm", BCM), ("cmi", CMI),
                         ("qrd", P.count("qrd"))], inc="qnm", inc_by=1)
                cs_ap = tab[:, tabo + t0:tabo + t0 + w]
                ss_ap = tab[:, tabo + 2560 + t0:tabo + 2560 + t0 + w]
                ro = reg + c * regw + (t0 - base)
                P.chain("vector", [
                    (lambda w, cs_ap: lambda e: e.tensor_mul(
                        qf[0:64, :w], qn[0:64, :w], cs_ap))(w, cs_ap),
                    (lambda w, cs_ap: lambda e: e.tensor_mul(
                        qhn[:, :w], qhn[:, :w], cs_ap))(w, cs_ap),
                    (lambda w, ss_ap: lambda e: e.tensor_mul(
                        qns[0:64, :w], qns[0:64, :w], ss_ap))(w, ss_ap),
                    (lambda w, ss_ap: lambda e: e.tensor_mul(
                        qhs[:, :w], qhs[:, :w], ss_ap))(w, ss_ap),
                    (lambda w, ro: lambda e: e.tensor_add(
                        BIGB[0:64, ro:ro + w], qf[0:64, :w], qns[0:64, :w]))(w, ro),
                    (lambda w, ro: lambda e: e.tensor_add(
                        BIGB[64:128, ro:ro + w], qhn[:, :w], qhs[:, :w]))(w, ro),
                ], wait=[("qnm", P.count("qnm")), ("cst", CST)], inc="qrd", inc_by=1)
    qk_phase("q")
    qk_phase("k")
    QRD_QK = P.count("qrd")
    QKM_ALL = P.count("qkm")
    QEV_ALL = P.count("qev")
    QNM_ALL = P.count("qnm")

    for half in range(2):
        p("gpsimd", (lambda half: lambda e: e.dma_start(
            out=wbuf[:, 0:4096].rearrange("p (k m) -> p k m", m=512),
            in_=wqkvp.rearrange("p (k m) -> p k m", m=3072)[
                :, :, 2048 + half * 512:2048 + (half + 1) * 512]))(half),
          wait=[("qkm", QKM_ALL), ("vmm", P.count("vmm"))], inc="wvl", inc_by=16)
        WVL = P.count("wvl")
        for vi in range(19):
            t0 = KO + vi * 128
            slot = (vi % 2) * VW
            pe = P.count("vev")
            ps = PSMM[vi % 2]
            items = [(xbs(k, t0, t0 + 128),
                      wbuf[:, k * 512:(k + 1) * 512], []) for k in range(NCH)]
            mm_group(P, ps[0:128, 0:512], items,
                     wait_first=[("wvl", WVL), ("xbl", XBL), ("qev", QEV_ALL),
                                 ("qnm", QNM_ALL)] +
                                ([("vev", pe)] if pe >= 1 else []),
                     inc="vmm", inc_by=1)
            VMM = P.count("vmm")
            so = slot + half * 520
            P.chain("vector", [
                (lambda so, ps: lambda e: e.tensor_copy(
                    vt[:, so:so + 520].rearrange("p (h f) -> p h f", f=65)[:, :, 0:64],
                    ps[:].rearrange("p (h f) -> p h f", f=64)))(so, ps),
                (lambda so: lambda e: e.memset(
                    vt[:, so + 64:so + 520:65], 1.0))(so),
            ], wait=[("vmm", VMM), ("vbd", max(0, P.count("vbd") - 16))],
               inc="vev", inc_by=1)
            p("sync", (lambda vi, so, half: lambda e: e.dma_start(
                out=vt_d[vi * 128:(vi + 1) * 128, half * 520:(half + 1) * 520],
                in_=vt[:, so:so + 520]))(vi, so, half),
              wait=[("vev", P.count("vev"))], inc="vbd", inc_by=16)
    VBD_ALL = P.count("vbd")
    if upto == "qk":
        p("sync", lambda e: e.dma_start(out=qr_t, in_=BIGB[:, QRO:QRO + NCH * NQ]),
          wait=[("qrd", QRD_QK)], inc="tapd", inc_by=16)
        p("sync", lambda e: e.dma_start(out=kr_t, in_=BIGB[:, KRO:KRO + NCH * NK]),
          inc="tapd", inc_by=16)
        p("sync", lambda e: e.dma_start(out=vt_t, in_=vt_d),
          wait=[("vbd", VBD_ALL)], inc="tapd", inc_by=16)
        P.wait("sync", [("tapd", P.count("tapd"))])
        return L

    loaded = {}
    for tq in range(ntq):
        kcol = _l1_kcol(tq)
        vi0 = kcol // 128
        for cc in range(4):
            vi = vi0 + cc
            if vi not in loaded:
                p("sync", (lambda vi: lambda e: e.dma_start(
                    out=vt[:, (vi % 6) * VW:(vi % 6) * VW + VW],
                    in_=vt_d[vi * 128:(vi + 1) * 128, :]))(vi),
                  wait=[("vbd", VBD_ALL), ("qrd", QRD_QK),
                        ("pvm", P.count("pvm"))],
                  inc="vrl", inc_by=16)
                loaded[vi] = P.count("vrl")
        if tq == 0:
            mko = 0
        elif tq == 1:
            mko = 2 * 512
        elif tq == 2:
            mko = 3 * 512
        else:
            mko = (1 - tq % 2) * 512
        for h in range(nhh):
            ch, rh = h // 2, (h % 2) * 64
            ps_s = PSMM[h % 2]
            pe = P.count("sev")
            p("tensor", (lambda ch, rh, tq, kcol, ps_s: lambda e: e.matmul(
                ps_s[:],
                BIGB[rh:rh + 64, QRO + ch * NQ + tq * 128:QRO + ch * NQ + (tq + 1) * 128],
                BIGB[rh:rh + 64, KRO + ch * NK + kcol:KRO + ch * NK + kcol + 512],
                start=True, stop=True, skip_group_check=True))(ch, rh, tq, kcol, ps_s),
              wait=[("qrd", QRD_QK), ("vbd", VBD_ALL),
                    ("pev", P.count("pev"))] +
                   ([("sev", max(0, pe - 1))] if pe >= 1 else []),
              inc="smm", inc_by=1)
            P.chain("vector", [
                (lambda ps_s, mko: lambda e: e.tensor_add(
                    sc_sb[:], ps_s[:], mks[:, mko:mko + 512]))(ps_s, mko),
                lambda e: e.reduce_max(mxc[:, 0:1], sc_sb[:], axis=AX.X),
                lambda e: e.tensor_scalar(mxc[:, 1:2], mxc[:, 0:1], -1.0, None, ALU.mult),
            ], wait=[("smm", P.count("smm")), ("sex", P.count("sex"))],
               inc="sev", inc_by=1)
            p("scalar", lambda e: e.activation(
                pt_sb[:], sc_sb[:], AF.Exp, bias=mxc[:, 1:2]),
              wait=[("sev", P.count("sev")), ("trm", P.count("trm"))],
              inc="sex", inc_by=1)
            SEX = P.count("sex")
            for cc in range(4):
                pt2 = P.count("trc")
                p("tensor", (lambda cc: lambda e: e.transpose(
                    ps_tr[:], pt_sb[:, cc * 128:(cc + 1) * 128],
                    identb[:]))(cc),
                  wait=[("sex", SEX), ("cst", CST)] +
                       ([("trc", pt2)] if pt2 >= 1 else []) +
                       ([("pvm", P.count("pvm"))] if cc == 0 and P.count("pvm") else []),
                  inc="trm", inc_by=1)
                P.chain("vector", [
                    (lambda cc: lambda e: e.tensor_copy(
                        ptT[:, cc * 128:(cc + 1) * 128], ps_tr[:]))(cc),
                ], wait=[("trm", P.count("trm"))], inc="trc", inc_by=1)
            TRC = P.count("trc")
            ps_p = ps_pv if h % 2 == 0 else ps_sc
            pe2 = P.count("pev")
            items = []
            for cc in range(4):
                vi = vi0 + cc
                hcol = 520 * (h // 8) + (h % 8) * 65
                items.append((vt[:, (vi % 6) * VW + hcol:(vi % 6) * VW + hcol + 65],
                              ptT[:, cc * 128:(cc + 1) * 128],
                              [("trc", TRC)] if cc == 0 else []))
            mm_group(P, ps_p[0:65, 0:128], items,
                     wait_first=[("vrl", max(loaded[vi0 + cc2] for cc2 in range(4)))] +
                                ([("pev", max(0, pe2 - 1))] if pe2 >= 1 else []),
                     inc="pvm", inc_by=1)
            PVM = P.count("pvm")
            P.chain("vector", [
                (lambda ps_p: lambda e: e.tensor_copy(srf[:], ps_p[64:65, 0:128]))(ps_p),
                lambda e: e.reciprocal(srf[:], srf[:]),
                lambda e: e.tensor_copy(srb[:], srf[:]),
            ], wait=[("pvm", PVM), ("nbm", P.count("nbm"))], inc="nrv", inc_by=1)
            p("tensor", lambda e: e.matmul(
                ps_bc1[:, 0:128], ones1b[:], srb[:],
                start=True, stop=True, skip_group_check=True),
              wait=[("nrv", P.count("nrv")), ("pev", P.count("pev"))],
              inc="nbm", inc_by=1)
            P.chain("vector", [
                (lambda ps_p: lambda e: e.tensor_copy(pv_sb[:], ps_p[0:64, 0:128]))(ps_p),
                (lambda ch, rh: lambda e: e.tensor_mul(
                    aot[rh:rh + 64, ch * 128:ch * 128 + 128], pv_sb[:],
                    ps_bc1[:, 0:128]))(ch, rh),
            ], wait=[("nbm", P.count("nbm")), ("pev", P.count("pev")),
                     ("aod", P.count("aod"))],
               inc="pev", inc_by=1)
        PEV = P.count("pev")
        p("sync", (lambda tq: lambda e: e.dma_start(
            out=ao_d.rearrange("(c p) n -> p c n", p=128)[:, :, tq * 128:(tq + 1) * 128],
            in_=aot[:].rearrange("p (c n) -> p c n", n=128)))(tq),
          wait=[("pev", PEV)], inc="aod", inc_by=16)
    AOD_ALL = P.count("aod")
    PVM_ALL = P.count("pvm")
    SMM_ALL = P.count("smm")
    if upto == "sc":
        p("sync", lambda e: e.dma_start(out=ao_t, in_=ao_d),
          wait=[("aod", AOD_ALL)], inc="tapd", inc_by=16)
        P.wait("sync", [("tapd", P.count("tapd"))])
        return L

    WOT = [512, 512, 512, 512, 128]
    STG = (qf, qn, qns)
    t0 = 0
    for tt, w in enumerate(WOT):
        pm = P.count("wmm")
        for k in range(NCH):
            p("gpsimd", (lambda k, t0, w: lambda e: e.dma_start(
                out=BIGB[:, k * 512:k * 512 + w],
                in_=ao_d[k * 128:(k + 1) * 128, t0:t0 + w]))(k, t0, w),
              wait=([("aod", AOD_ALL), ("smm", SMM_ALL)] if k == 0 else []) +
                   ([("wmm", pm)] if (k == 0 and pm >= 1) else []),
              inc="aol", inc_by=16)
        AOL = P.count("aol")
        for c in range(NCH):
            slot = c * 520
            pxe = P.count("xpe")
            p("sync", (lambda c, slot, t0, w: lambda e: e.dma_start(
                out=xr[:, slot:slot + w],
                in_=xin[c * 128:(c + 1) * 128, QO + t0:QO + t0 + w]))(c, slot, t0, w),
              wait=[("xpe", pxe)] if (c == 0 and pxe >= 1) else [],
              inc="xrl", inc_by=16)
        XRL = P.count("xrl")
        for c in range(NCH):
            gi = tt * NCH + c
            wslot = (c % 2) * 2816
            pw = P.count("wmm")
            p("gpsimd", (lambda c, wslot: lambda e: e.dma_start(
                out=wbuf[:, wslot:wslot + 1024].rearrange("p (k m) -> p k m", m=128),
                in_=wop.rearrange("p (k m) -> p k m", m=128)[:, :, c * 128:(c + 1) * 128]
                if False else
                wop.rearrange("p (k m) -> p k m", m=1024)[:, :, c * 128:(c + 1) * 128]))(c, wslot),
              wait=[("vmm", P.count("vmm"))] if gi == 0 else
                   [("wmm", max(0, pw - 1))],
              inc="wol", inc_by=16)
            WOL = P.count("wol")
            ps = PSMM[gi % 2]
            pe = P.count("xpe")
            items = [(wbuf[:, wslot + k * 128:wslot + (k + 1) * 128],
                      BIGB[:, k * 512:k * 512 + w], []) for k in range(NCH)]
            mm_group(P, ps[:, :w], items,
                     wait_first=[("wol", WOL), ("aol", AOL)] +
                                ([("xpe", pe)] if pe >= 1 else []),
                     inc="wmm", inc_by=1)
            WMM = P.count("wmm")
            stg = STG[gi % 3]
            slot = c * 520
            nd = 2 if upto in ("attn", "both") else 1
            P.chain("vector", [
                (lambda ps, stg, slot, w: lambda e: e.tensor_add(
                    stg[:, :w], ps[:, :w], xr[:, slot:slot + w]))(ps, stg, slot, w),
            ], wait=[("wmm", WMM), ("xrl", XRL),
                     ("xpd", max(0, P.count("xpd") - nd * 16 * 2))],
               inc="xpe", inc_by=1)
            XPE = P.count("xpe")
            p("sync", (lambda c, stg, t0, w: lambda e: e.dma_start(
                out=xp_d[c * 128:(c + 1) * 128, t0:t0 + w], in_=stg[:, :w]))(c, stg, t0, w),
              wait=[("xpe", XPE)], inc="xpd", inc_by=16)
            if upto in ("attn", "both"):
                p("sync", (lambda c, stg, t0, w: lambda e: e.dma_start(
                    out=xp_t[c * 128:(c + 1) * 128, t0:t0 + w], in_=stg[:, :w]))(c, stg, t0, w),
                  inc="xpd", inc_by=16)
        t0 += w
    XPD_ALL = P.count("xpd")
    if upto == "attn":
        P.wait("sync", [("xpd", XPD_ALL)])
        return L

    for tt4 in range(4):
        t0w = tt4 * 512
        xcol = 127 + t0w
        pxn = P.count("xnw")
        for c in range(NCH):
            slot = c * 520
            p("sync", (lambda c, slot, xcol: lambda e: e.dma_start(
                out=xr[:, slot:slot + 513],
                in_=xp_d[c * 128:(c + 1) * 128, xcol:xcol + 513]))(c, slot, xcol),
              wait=([("xpd", XPD_ALL)] if tt4 == 0 and c == 0 else []) +
                   ([("w2e", P.count("w2e")), ("xnw", P.count("xnw")),
                     ("lsm", P.count("lsm"))] if tt4 > 0 and c == 0 else []),
              inc="xfl", inc_by=16)
        XFL = P.count("xfl")

        def xrsh(c, tt4):
            slot = c * 520
            off = 1 if c < 4 else 0
            return xr[:, slot + off:slot + off + 512]

        def xrun(c, tt4):
            slot = c * 520
            return xr[:, slot + 1:slot + 513]

        for c in range(NCH):
            P.chain("vector", [
                (lambda c, tt4: lambda e: e.tensor_copy(
                    BIGB[:, FXS + c * 512:FXS + (c + 1) * 512], xrsh(c, tt4)))(c, tt4),
            ], wait=[("xfl", XFL)] +
                    ([("w1m", P.count("w1m")), ("xod", P.count("xod"))]
                     if c == 0 and tt4 > 0 else []),
               inc="xsb", inc_by=1)
            p("scalar", (lambda c: lambda e: e.activation(
                qsqb[:], BIGB[:, FXS + c * 512:FXS + (c + 1) * 512], AF.Square))(c),
              wait=[("xsb", P.count("xsb")), ("lsm", P.count("lsm"))],
              inc="lsq", inc_by=1)
            p("tensor", (lambda c: lambda e: e.matmul(
                ps_s2[0:1, :], onesb[:], BIGB[:, FXS + c * 512:FXS + (c + 1) * 512],
                start=(c == 0), stop=(c == NCH - 1), skip_group_check=True))(c),
              wait=[("xsb", P.count("xsb"))] +
                   ([("lr1", P.count("lr1"))] if c == 0 else []))
            p("tensor", (lambda c: lambda e: e.matmul(
                ps_bc1[0:1, :], onesb[:], qsqb[:],
                start=(c == 0), stop=(c == NCH - 1), skip_group_check=True))(c),
              wait=[("lsq", P.count("lsq"))] +
                   ([("lr1", P.count("lr1")), ("nbm", P.count("nbm"))] if c == 0 else []),
              inc="lsm", inc_by=1)
        LSM = P.count("lsm")
        P.chain("vector", [
            lambda e: e.tensor_copy(nr2[0:1, :], ps_s2[0:1, :]),
            lambda e: e.tensor_scalar(nr2[0:1, :], nr2[0:1, :],
                                      1.0 / D, None, ALU.mult),
            lambda e: e.tensor_mul(rows3[0:1, 0:512], nr2[0:1, :],
                                   nr2[0:1, :]),
            lambda e: e.scalar_tensor_tensor(rows3[0:1, 0:512],
                                             ps_bc1[0:1, :], 1.0 / D,
                                             rows3[0:1, 0:512],
                                             ALU.mult, ALU.subtract),
            lambda e: e.tensor_scalar(rows3[0:1, 0:512], rows3[0:1, 0:512],
                                      EPS, None, ALU.add),
        ], wait=[("lsm", LSM)], inc="lr1", inc_by=1)
        p("scalar", lambda e: e.activation(rows3[0:1, 0:512],
                                           rows3[0:1, 0:512], AF.Sqrt),
          wait=[("lr1", P.count("lr1"))], inc="lr2", inc_by=1)
        P.chain("vector", [
            lambda e: e.reciprocal(rows3[0:1, 0:512], rows3[0:1, 0:512]),
            lambda e: e.scalar_tensor_tensor(nr2[0:1, :], nr2[0:1, :],
                                             -1.0, rows3[0:1, 0:512],
                                             ALU.mult, ALU.mult),
        ], wait=[("lr2", P.count("lr2"))], inc="lrc", inc_by=1)
        p("sync", lambda e: e.dma_start(out=scr_ln[0:1, 0:512],
                                        in_=rows3[0:1, 0:512]),
          wait=[("lrc", P.count("lrc"))], inc="lbd", inc_by=16)
        p("sync", lambda e: e.dma_start(out=scr_ln[0:1, 512:1024],
                                        in_=nr2[0:1, :]),
          inc="lbd", inc_by=16)
        pbw = P.count("xnw")
        p("sync", lambda e: e.dma_start(out=Rbc[:],
                                        in_=scr_ln[0:1, 0:512].broadcast_to((128, 512))),
          wait=[("lbd", P.count("lbd"))] +
               ([("xnw", pbw)] if pbw >= 1 else []),
          inc="lbc", inc_by=16)
        p("sync", lambda e: e.dma_start(out=Bbc[:],
                                        in_=scr_ln[0:1, 512:1024].broadcast_to((128, 512))),
          inc="lbc", inc_by=16)
        LBC = P.count("lbc")
        for c in range(NCH):
            P.chain("vector", [
                (lambda c, tt4: lambda e: e.tensor_mul(sc_sb[:], xrsh(c, tt4), Rbc[:]))(c, tt4),
                lambda e: e.tensor_add(sc_sb[:], sc_sb[:], Bbc[:]),
                (lambda c: lambda e: e.tensor_scalar(
                    BIGB[:, FXN + c * 512:FXN + (c + 1) * 512], sc_sb[:],
                    gbL[:, c:c + 1], bbL[:, c:c + 1], ALU.mult, ALU.add))(c),
            ], wait=[("lbc", LBC), ("csf", CSF), ("xnw", P.count("xnw"))] +
                    ([("xod", P.count("xod"))] if c == 0 and tt4 > 0 else []),
               inc="xnw", inc_by=1)
        XNW = P.count("xnw")
        P.chain("vector", [
            lambda e: e.memset(BIGB[32:64, FH + 21 * 512:FH + 22 * 512], 0.0),
            lambda e: e.memset(BIGB[64:128, FH + 21 * 512:FH + 22 * 512], 0.0),
        ], wait=[("w2m", P.count("w2m"))] if tt4 > 0 else [], inc="hpz", inc_by=1)
        HPZ = P.count("hpz")
        for j in range(NJ):
            na = 42 if j == NJ - 1 else 128
            wslot = (j % 2) * 2816
            pm1 = P.count("w1m")
            p("gpsimd", (lambda j, na, wslot: lambda e: e.dma_start(
                out=wbuf[:, wslot:wslot + 8 * na].rearrange("p (k m) -> p k m", m=na),
                in_=w1p.rearrange("p (k m) -> p k m", m=5460)[:, :, j * 128:j * 128 + na]))(j, na, wslot),
              wait=[("w1m", max(0, pm1 - 2))] +
                   ([("qkm", QKM_ALL), ("wmm", P.count("wmm"))]
                    if tt4 == 0 and j == 0 else []) +
                   ([("w2m", P.count("w2m"))] if tt4 > 0 and j < 2 else []),
              inc="w1l", inc_by=16)
            p("gpsimd", (lambda j, na, wslot: lambda e: e.dma_start(
                out=wbuf[:, wslot + 1408:wslot + 1408 + 8 * na].rearrange(
                    "p (k m) -> p k m", m=na),
                in_=w1p.rearrange("p (k m) -> p k m", m=5460)[:, :, 2730 + j * 128:2730 + j * 128 + na]))(j, na, wslot),
              inc="w1l", inc_by=16)
            W1L = P.count("w1l")
            psA = (ps_mm0, ps_sc)[j % 2]
            psB = (ps_mm1, ps_pv)[j % 2]
            pge = P.count("gev")
            itA = [(wbuf[:, wslot + k * na:wslot + (k + 1) * na],
                    BIGB[:, FXN + k * 512:FXN + (k + 1) * 512], []) for k in range(NCH)]
            itB = [(wbuf[:, wslot + 1408 + k * na:wslot + 1408 + (k + 1) * na],
                    BIGB[:, FXN + k * 512:FXN + (k + 1) * 512], []) for k in range(NCH)]
            mm_group(P, psA[0:na, :], itA,
                     wait_first=[("w1l", W1L), ("xnw", XNW)] +
                                ([("gev", pge), ("hw", P.count("hw"))] if pge >= 1 else []),
                     inc="w1m", inc_by=1)
            mm_group(P, psB[0:na, :], itB, wait_first=[], inc="w1m", inc_by=1)
            W1M = P.count("w1m")
            ho = FH + j * 512
            P.chain("vector", [
                (lambda psA, na, j: lambda e: e.tensor_scalar(
                    qf[0:na, :], psA[0:na, :], b1ab[0:na, j:j + 1], None, ALU.add))(psA, na, j),
                (lambda psB, na, j: lambda e: e.tensor_scalar(
                    qn[0:na, :], psB[0:na, :], b1gb[0:na, j:j + 1], None, ALU.add))(psB, na, j),
            ], wait=[("w1m", W1M), ("ger", P.count("ger")),
                     ("hw", P.count("hw"))], inc="gev", inc_by=1)
            p("scalar", (lambda na: lambda e: e.activation(
                qns[0:na, :], qn[0:na, :], AF.Gelu))(na),
              wait=[("gev", P.count("gev"))], inc="ger", inc_by=1)
            P.chain("vector", [
                (lambda na, ho: lambda e: e.tensor_mul(
                    BIGB[0:na, ho:ho + 512], qns[0:na, :], qf[0:na, :]))(na, ho),
            ], wait=[("ger", P.count("ger")), ("hpz", HPZ)] +
                    ([("w2m", P.count("w2m"))] if j < 2 and tt4 > 0 else []),
               inc="hw", inc_by=1)
        HW = P.count("hw")
        for o in range(NCH):
            wslot = (o % 2) * 2816
            pm2 = P.count("w2m")
            p("gpsimd", (lambda o, wslot: lambda e: e.dma_start(
                out=wbuf[:, wslot:wslot + 2816].rearrange("p (k m) -> p k m", m=128),
                in_=w2pk.rearrange("p (k m) -> p k m", m=1024)[:, :, o * 128:(o + 1) * 128]))(o, wslot),
              wait=[("w1m", P.count("w1m"))] if o < 2 else
                   [("w2m", max(0, pm2 - 1))],
              inc="w2l", inc_by=16)
            W2L = P.count("w2l")
            ps = PSMM[o % 2]
            pe = P.count("w2e")
            items = [(wbuf[:, wslot + k * 128:wslot + (k + 1) * 128],
                      BIGB[:, FH + k * 512:FH + (k + 1) * 512], []) for k in range(NJ)]
            mm_group(P, ps[:], items,
                     wait_first=[("w2l", W2L), ("hw", HW)] +
                                ([("w2e", pe), ("xod", P.count("xod"))] if pe >= 1 else []),
                     inc="w2m", inc_by=1)
            W2M = P.count("w2m")
            pxo = P.count("xod")
            P.chain("vector", [
                (lambda ps, o, tt4: lambda e: e.tensor_scalar(
                    sc_sb[:], ps[:], b2b[:, o:o + 1], None, ALU.add))(ps, o, tt4),
                (lambda o, tt4: lambda e: e.tensor_add(
                    xosf[:, o * 512:(o + 1) * 512], sc_sb[:], xrun(o, tt4)))(o, tt4),
            ], wait=[("w2m", W2M), ("w2e", P.count("w2e")),
                     ("xod", max(0, pxo - 16))], inc="w2e", inc_by=1)
            W2E = P.count("w2e")
            p("sync", (lambda o, t0w: lambda e: e.dma_start(
                out=xo[o * 128:(o + 1) * 128, t0w:t0w + 512],
                in_=xosf[:, o * 512:(o + 1) * 512]))(o, t0w),
              wait=[("w2e", W2E)], inc="xod", inc_by=16)
    P.wait("sync", [("xod", P.count("xod"))])
    return L



def build_lnsand(ncols, residual):
    L = LaunchBuilder()
    P = L.P
    p = P.op
    xi = L.inp("xi", [D, ncols], F32)
    if residual:
        xres = L.inp("xres", [D, ncols], F32)
    yo = L.out("yo", [D, ncols], F32)
    xt = L.sb("xtS", [128, NCH * 512], F32)
    xrt = L.sb("xrtS", [128, NCH * 512], F32) if residual else None
    xbt = L.sb("xbtS", [128, NCH * 512], BF16)
    sqp = L.sb("sqpS", [128, 2 * 512], BF16)
    rowA = L.sb("rowAS", [1, 512], F32)
    rowB = L.sb("rowBS", [1, 512], F32)
    Rbc = L.sb("RbcS", [128, 512], F32)
    Bbc = L.sb("BbcS", [128, 512], F32)
    onesb = L.sb("onesbS", [128, 1], BF16)
    scr = L.dram("scrS", [1, 1024], F32)
    ps_s = L.ps("ps_sS", [1, 512])
    ps_q = L.ps("ps_qS", [1, 512])
    P.op("vector", lambda e: e.memset(onesb[:], 1.0), inc="ini", inc_by=1)
    INI = P.count("ini")
    ntile = cdiv(ncols, 512)
    for tt in range(ntile):
        t0 = tt * 512
        w = min(512, ncols - t0)
        pxa = P.count("apl")
        for c in range(NCH):
            p("sync", (lambda c, t0, w: lambda e: e.dma_start(
                out=xt[:, c * 512:c * 512 + w],
                in_=xi[c * 128:(c + 1) * 128, t0:t0 + w]))(c, t0, w),
              wait=[("apl", pxa), ("yod", P.count("yod"))] if (c == 0 and tt > 0) else [],
              inc="xld", inc_by=16)
            if residual:
                p("sync", (lambda c, t0, w: lambda e: e.dma_start(
                    out=xrt[:, c * 512:c * 512 + w],
                    in_=xres[c * 128:(c + 1) * 128, t0:t0 + w]))(c, t0, w),
                  inc="xld", inc_by=16)
        XLD = P.count("xld")
        for c in range(NCH):
            P.chain("vector", [
                (lambda c, w: lambda e: e.tensor_copy(
                    xbt[:, c * 512:c * 512 + w], xt[:, c * 512:c * 512 + w]))(c, w),
            ], wait=[("xld", XLD)] +
                    ([("rws", P.count("rws"))] if c == 0 and tt > 0 else []),
               inc="xbc", inc_by=1)
            pp = (tt * NCH + c) % 2
            p("scalar", (lambda c, w, pp: lambda e: e.activation(
                sqp[:, pp * 512:pp * 512 + w], xbt[:, c * 512:c * 512 + w],
                AF.Square))(c, w, pp),
              wait=[("xbc", P.count("xbc")), ("sqm", max(0, P.count("sqm") - 1))],
              inc="sqd", inc_by=1)
            p("tensor", (lambda c, w: lambda e: e.matmul(
                ps_s[:, :w], onesb[:], xbt[:, c * 512:c * 512 + w],
                start=(c == 0), stop=(c == NCH - 1), skip_group_check=True))(c, w),
              wait=[("xbc", P.count("xbc")), ("ini", INI)] +
                   ([("rws", P.count("rws"))] if c == 0 and tt > 0 else []))
            p("tensor", (lambda c, w, pp: lambda e: e.matmul(
                ps_q[:, :w], onesb[:], sqp[:, pp * 512:pp * 512 + w],
                start=(c == 0), stop=(c == NCH - 1), skip_group_check=True))(c, w, pp),
              wait=[("sqd", P.count("sqd"))], inc="sqm", inc_by=1)
        SQM = P.count("sqm")
        P.chain("vector", [
            (lambda w: lambda e: e.tensor_copy(rowA[0:1, :w], ps_s[:, :w]))(w),
            (lambda w: lambda e: e.tensor_scalar(
                rowA[0:1, :w], rowA[0:1, :w], 1.0 / D, None, ALU.mult))(w),
            (lambda w: lambda e: e.tensor_mul(
                rowB[0:1, :w], rowA[0:1, :w], rowA[0:1, :w]))(w),
            (lambda w: lambda e: e.scalar_tensor_tensor(
                rowB[0:1, :w], ps_q[:, :w], 1.0 / D, rowB[0:1, :w],
                ALU.mult, ALU.subtract))(w),
            (lambda w: lambda e: e.tensor_scalar(
                rowB[0:1, :w], rowB[0:1, :w], EPS, None, ALU.add))(w),
        ], wait=[("sqm", SQM)], inc="rw1", inc_by=1)
        p("scalar", (lambda w: lambda e: e.activation(
            rowB[0:1, :w], rowB[0:1, :w], AF.Sqrt))(w),
          wait=[("rw1", P.count("rw1"))], inc="rw2", inc_by=1)
        P.chain("vector", [
            (lambda w: lambda e: e.reciprocal(rowB[0:1, :w], rowB[0:1, :w]))(w),
            (lambda w: lambda e: e.scalar_tensor_tensor(
                rowA[0:1, :w], rowA[0:1, :w], -1.0, rowB[0:1, :w],
                ALU.mult, ALU.mult))(w),
        ], wait=[("rw2", P.count("rw2"))], inc="rws", inc_by=1)
        p("sync", (lambda w: lambda e: e.dma_start(
            out=scr[0:1, 0:w], in_=rowB[0:1, :w]))(w),
          wait=[("rws", P.count("rws")), ("bcl", P.count("bcl"))],
          inc="rbd", inc_by=16)
        p("sync", (lambda w: lambda e: e.dma_start(
            out=scr[0:1, 512:512 + w], in_=rowA[0:1, :w]))(w),
          inc="rbd", inc_by=16)
        pa = P.count("apl")
        p("sync", (lambda w: lambda e: e.dma_start(
            out=Rbc[:, :w], in_=scr[0:1, 0:w].broadcast_to((128, w))))(w),
          wait=[("rbd", P.count("rbd"))] +
               ([("apl", pa)] if tt > 0 else []),
          inc="bcl", inc_by=16)
        p("sync", (lambda w: lambda e: e.dma_start(
            out=Bbc[:, :w], in_=scr[0:1, 512:512 + w].broadcast_to((128, w))))(w),
          inc="bcl", inc_by=16)
        BCL = P.count("bcl")
        for c in range(NCH):
            ops = [
                (lambda c, w: lambda e: e.tensor_mul(
                    xt[:, c * 512:c * 512 + w], xt[:, c * 512:c * 512 + w],
                    Rbc[:, :w]))(c, w),
                (lambda c, w: lambda e: e.tensor_add(
                    xt[:, c * 512:c * 512 + w], xt[:, c * 512:c * 512 + w],
                    Bbc[:, :w]))(c, w),
            ]
            if residual:
                ops.append((lambda c, w: lambda e: e.tensor_add(
                    xt[:, c * 512:c * 512 + w], xt[:, c * 512:c * 512 + w],
                    xrt[:, c * 512:c * 512 + w]))(c, w))
            P.chain("vector", ops,
                    wait=[("bcl", BCL),
                          ("yod", max(0, P.count("yod") - 16 * NCH))],
                    inc="apl", inc_by=1)
            APL = P.count("apl")
            p("sync", (lambda c, t0, w: lambda e: e.dma_start(
                out=yo[c * 128:(c + 1) * 128, t0:t0 + w],
                in_=xt[:, c * 512:c * 512 + w]))(c, t0, w),
              wait=[("apl", APL)], inc="yod", inc_by=16)
    P.wait("sync", [("yod", P.count("yod"))])
    return L



_IDENT = np.eye(128, dtype=np.float32)


def to_fm(x):
    return np.ascontiguousarray(x.reshape(-1, D).T)


def from_fm(xfm):
    return np.ascontiguousarray(xfm.T).reshape(BATCH, T, D)


_cache = {}


def run_ffn0(x1_fm, inputs):
    if "ffn0" not in _cache:
        _cache["ffn0"] = build_ffn0().finish()
    nc = _cache["ffn0"]
    in_maps = []
    for c in range(NCORE):
        in_maps.append({
            "x1": np.ascontiguousarray(x1_fm[:, c * TC:(c + 1) * TC]),
            "kern": inputs["L0_ffn_kernel"],
            "biasv": inputs["L0_ffn_bias"].reshape(1, D),
            "amat": inputs["L0_ffn_a"],
            "bmat": inputs["L0_ffn_b"],
        })
    res = run_bass_kernel_spmd(nc, in_maps, core_ids=list(range(NCORE)))
    out = np.concatenate([res.results[c]["x2"] for c in range(NCORE)], axis=1)
    return out




def attn0_masks(p):
    tri = (np.arange(128)[:, None] <= np.arange(128)[None, :]).astype(np.float32)
    zero = np.zeros((128, 128), np.float32)
    one = np.ones((128, 128), np.float32)
    m0 = np.zeros((128, 256), np.float32) if p == 0 else np.ones((128, 256), np.float32)
    m1 = np.concatenate([zero, tri], axis=1)
    m2 = np.concatenate([zero, one], axis=1)
    m3 = np.concatenate([tri, one], axis=1)
    return np.ascontiguousarray(np.concatenate([m0, m1, m2, m3], axis=1))


def attn0_in_maps(x_fm, inputs):
    qkvT = np.ascontiguousarray(inputs["L0_attn_qkv_w"].T)
    outT = np.ascontiguousarray(inputs["L0_attn_out_w"].T)
    hexT = np.ascontiguousarray(inputs["L0_attn_hex_w"].T)
    q6T = np.ascontiguousarray(inputs["L0_attn_q6_w"].T)
    trf = np.ascontiguousarray(inputs["L0_attn_transforms"].transpose(0, 2, 1).reshape(7, 36))
    trf = np.ascontiguousarray(inputs["L0_attn_transforms"].reshape(7, 36))
    tw = inputs["L0_attn_transform_weights"].astype(np.float32)
    smalls = np.array([[float(inputs["L0_attn_hamming_lambda_logit"]),
                        float(inputs["L0_attn_scale_logit"]),
                        float(np.asarray(inputs["L0_attn_sips_alpha"]).ravel()[0]),
                        float(np.asarray(inputs["L0_attn_sips_beta"]).ravel()[0])]], np.float32)
    maps = []
    for c in range(NCORE):
        b, p = c // 2, c % 2
        xb_ = x_fm[:, b * T:(b + 1) * T]
        if p == 0:
            xc = np.concatenate([np.zeros((D, 128), np.float32), xb_[:, :T - 128]], axis=1)
        else:
            xc = xb_
        gcols = np.concatenate([np.arange((2 * j + p) * 128, (2 * j + p + 1) * 128)
                                for j in range(16)])
        xmy = xb_[:, gcols]
        maps.append({
            "xc": np.ascontiguousarray(xc), "xmy": np.ascontiguousarray(xmy),
            "qkvT": qkvT, "outT": outT, "hexT": hexT, "q6T": q6T,
            "trf": trf, "tw": tw, "smalls": smalls,
            "masksI": attn0_masks(p),
        })
    return maps


def attn0_scatter(results, x1_fm_out):
    for c in range(NCORE):
        b, p = c // 2, c % 2
        gcols = np.concatenate([np.arange((2 * j + p) * 128, (2 * j + p + 1) * 128)
                                for j in range(16)])
        x1_fm_out[:, b * T + gcols] = results[c]["x1t"]
    return x1_fm_out


def run_attn0(x_fm, inputs):
    if "attn0" not in _cache:
        _cache["attn0"] = build_attn0().finish()
    nc = _cache["attn0"]
    res = run_bass_kernel_spmd(nc, attn0_in_maps(x_fm, inputs), core_ids=list(range(NCORE)))
    out = np.empty((D, BATCH * T), np.float32)
    return attn0_scatter(res.results, out)




def mamba_in_maps(x2_fm, inputs):
    w_inT = np.ascontiguousarray(inputs["L1_attn_W_in"].T)
    convw = np.zeros((128, 17 * 4), np.float32)
    cw = inputs["L1_attn_conv_w"].reshape(2176, 4)
    cb = inputs["L1_attn_conv_b"]
    convb = np.zeros((128, 17), np.float32)
    for ch in range(17):
        convw[:, ch * 4:(ch + 1) * 4] = cw[ch * 128:(ch + 1) * 128]
        convb[:, ch] = cb[ch * 128:(ch + 1) * 128]
    trin = np.where(np.arange(64)[:, None] >= np.arange(64)[None, :], 0.0, -1e9).astype(np.float32)
    nwT = np.ascontiguousarray(inputs["L1_attn_norm_w"].reshape(16, 128).T)
    maps = []
    for c in range(NCORE):
        b = c // 2
        maps.append({
            "x2": np.ascontiguousarray(x2_fm[:, b * T:(b + 1) * T]),
            "w_inT": w_inT,
            "convw": convw, "convb": convb,
            "dtb": inputs["L1_attn_dt_bias"].reshape(32, 1),
            "alog": inputs["L1_attn_A_log"].reshape(32, 1),
            "dvec": inputs["L1_attn_D"].reshape(1, 32),
            "nwT": nwT,
            "w_outT": np.ascontiguousarray(inputs["L1_attn_W_out"].T),
            "identf": _IDENT,
            "trin": trin,
        })
    return maps


def run_mamba(x2_fm, inputs):
    if "mamba" not in _cache:
        _cache["mamba"] = build_mamba().finish()
    nc = _cache["mamba"]
    res = run_bass_kernel_spmd(nc, mamba_in_maps(x2_fm, inputs), core_ids=list(range(NCORE)))
    out = np.empty((D, BATCH * T), np.float32)
    for b in range(BATCH):
        out[:, b * T:(b + 1) * T] = res.results[2 * b]["x3"]
    return out


def l1_tables(p):
    half = np.arange(0, 64, 2, dtype=np.float32)
    inv_freq = (1.0 / 10000.0 ** (half / 64.0)).astype(np.float32)
    base_scale = ((half + 0.4 * 64) / (1.4 * 64)).astype(np.float32)
    g = p * 2048 - 512 + np.arange(2560, dtype=np.float32)
    fr = g[:, None] * inv_freq[None, :]
    cs = np.cos(fr).astype(np.float32)
    sn = np.sin(fr).astype(np.float32)
    power = (g - 2048.0) / 128.0
    scale = (base_scale[None, :] ** power[:, None]).astype(np.float32)
    inv = (1.0 / scale).astype(np.float32)
    valid = (g >= 0).astype(np.float32)[:, None]
    cs, sn, scale, inv = cs * valid, sn * valid, scale * valid, inv * valid

    def expand(m):
        return np.ascontiguousarray(np.concatenate([m.T, m.T], axis=0))
    sgn = np.concatenate([-np.ones((32, 1), np.float32),
                          np.ones((32, 1), np.float32)], axis=0)
    csq = expand(cs * scale)
    ssq = expand(sn * scale) * sgn
    csk = expand(cs * inv)
    ssk = expand(sn * inv) * sgn
    return np.ascontiguousarray(np.concatenate([csq, ssq, csk, ssk], axis=1))


def l1_masks(p):
    i = np.arange(128)[:, None]
    j = np.arange(512)[None, :]
    me = np.where((j >= i) & (j <= i + 256), 0.0, -1e9).astype(np.float32)
    mo = np.where((j >= 128 + i) & (j <= 384 + i), 0.0, -1e9).astype(np.float32)
    if p == 0:
        ban = np.where(j < 256, -1e9, 0.0).astype(np.float32)
        mkA = np.minimum(me, me + ban)
        mkB = np.minimum(mo, mo + ban)
        mkA = np.where(j < 256, -1e9, me).astype(np.float32)
        mkB = np.where(j < 256, -1e9, mo).astype(np.float32)
    else:
        mkA, mkB = me, mo
    return np.ascontiguousarray(np.concatenate([me, mo, mkA, mkB], axis=1))


def _pack_rows(w, kchunks):
    M = w.shape[1]
    return np.ascontiguousarray(
        w.reshape(kchunks, 128, M).transpose(1, 0, 2).reshape(128, kchunks * M))


def _col_chunks(v, n):
    out = np.zeros((128, n), np.float32)
    for c in range(n):
        seg = v[c * 128:(c + 1) * 128]
        out[:len(seg), c] = seg
    return out


def l1ffn_layer_inputs(inputs, i):
    W2 = np.asarray(inputs["L1_ffn_W2"][i], np.float32)
    W2p = np.zeros((2816, D), np.float32)
    W2p[:2730] = W2
    b1 = np.asarray(inputs["L1_ffn_b1"][i], np.float32)
    b1a = np.zeros((128, 22), np.float32)
    b1g = np.zeros((128, 22), np.float32)
    for j in range(22):
        na = 42 if j == 21 else 128
        b1a[:na, j] = b1[j * 128:j * 128 + na]
        b1g[:na, j] = b1[2730 + j * 128:2730 + j * 128 + na]
    return {
        "wqkvp": _pack_rows(np.asarray(inputs["L1_ffn_Wqkv"][i], np.float32), 8),
        "wop": _pack_rows(np.asarray(inputs["L1_ffn_Wo"][i], np.float32), 8),
        "w1p": _pack_rows(np.asarray(inputs["L1_ffn_W1"][i], np.float32), 8),
        "w2pk": _pack_rows(W2p, 22),
        "b1a": b1a, "b1g": b1g,
        "b2c": _col_chunks(np.asarray(inputs["L1_ffn_b2"][i], np.float32), 8),
        "gcol": _col_chunks(np.asarray(inputs["L1_ffn_ln_g"][i], np.float32), 8),
        "bcol": _col_chunks(np.asarray(inputs["L1_ffn_ln_b"][i], np.float32), 8),
        "qsc": np.ascontiguousarray(
            np.asarray(inputs["L1_ffn_q_scale"][i], np.float32).reshape(64, 1)),
        "ksc": np.ascontiguousarray(
            np.asarray(inputs["L1_ffn_k_scale"][i], np.float32).reshape(64, 1)),
        "identf": _IDENT,
    }


def _halo_slice(s_full, b, p):
    if p == 0:
        return np.ascontiguousarray(
            np.concatenate([np.zeros((D, 512), np.float32),
                            s_full[:, 0:TC]], axis=1))
    return np.ascontiguousarray(s_full[:, 1536:4096])


def run_l1ffn_layer(s_fulls, inputs, i):
    if "l1ffn" not in _cache:
        _cache["l1ffn"] = build_l1ffn().finish()
    nc = _cache["l1ffn"]
    lay = l1ffn_layer_inputs(inputs, i)
    maps = []
    for c in range(NCORE):
        b, p = c // 2, c % 2
        m = {"xin": _halo_slice(s_fulls[b], b, p),
             "tabs": _L1TABS[p], "masksL": _L1MASKS[p]}
        m.update(lay)
        maps.append(m)
    res = run_bass_kernel_spmd(nc, maps, core_ids=list(range(NCORE)))
    outs = []
    for b in range(BATCH):
        outs.append(np.concatenate(
            [res.results[2 * b]["xo"], res.results[2 * b + 1]["xo"]], axis=1))
    return outs


_L1TABS = None
_L1MASKS = None


def run_l1ffn_stack(x3_fm, inputs):
    global _L1TABS, _L1MASKS
    if _L1TABS is None:
        _L1TABS = [l1_tables(0), l1_tables(1)]
        _L1MASKS = [l1_masks(0), l1_masks(1)]
    if "lnpre" not in _cache:
        _cache["lnpre"] = build_lnsand(2560, False).finish()
    maps = []
    for c in range(NCORE):
        b, p = c // 2, c % 2
        maps.append({"xi": _halo_slice(x3_fm[:, b * T:(b + 1) * T], b, p)})
    res = run_bass_kernel_spmd(_cache["lnpre"], maps, core_ids=list(range(NCORE)))
    s_fulls = []
    for b in range(BATCH):
        s_fulls.append(np.concatenate(
            [res.results[2 * b]["yo"][:, 512:2560],
             res.results[2 * b + 1]["yo"][:, 512:2560]], axis=1))
    for i in range(6):
        s_fulls = run_l1ffn_layer(s_fulls, inputs, i)
    if "lnpost" not in _cache:
        _cache["lnpost"] = build_lnsand(2048, True).finish()
    maps = []
    for c in range(NCORE):
        b, p = c // 2, c % 2
        maps.append({
            "xi": np.ascontiguousarray(s_fulls[b][:, p * TC:(p + 1) * TC]),
            "xres": np.ascontiguousarray(
                x3_fm[:, b * T + p * TC:b * T + (p + 1) * TC]),
        })
    res = run_bass_kernel_spmd(_cache["lnpost"], maps, core_ids=list(range(NCORE)))
    out = np.empty((D, BATCH * T), np.float32)
    for c in range(NCORE):
        b, p = c // 2, c % 2
        out[:, b * T + p * TC:b * T + (p + 1) * TC] = res.results[c]["yo"]
    return out


def kernel(**inputs):
    x_fm = to_fm(np.asarray(inputs["L0_attn_x"], np.float32))
    x1 = run_attn0(x_fm, inputs)
    x2 = run_ffn0(x1, inputs)
    x3 = run_mamba(x2, inputs)
    x4 = run_l1ffn_stack(x3, inputs)
    return from_fm(x4)
